# version 16
# baseline (speedup 1.0000x reference)
import math
from contextlib import ExitStack

import numpy as np
import ml_dtypes

import concourse.bass as bass
import concourse.mybir as mybir
from concourse.bass_utils import run_bass_kernel_spmd

F32 = mybir.dt.float32
BF16 = mybir.dt.bfloat16
I32 = mybir.dt.int32
AF = mybir.ActivationFunctionType
ALU = mybir.AluOpType

D = 2048
S = 4096
T = 1024
DC = D // 128
NHL = 4
INTER = 5632
IT = INTER // 128
HT = DC
RK = 16
SCALE = 1.0 / math.sqrt(128.0)


class Sem:
    def __init__(self, nc, stack, name):
        self.h = stack.enter_context(nc.semaphore(name))
        self.v = 0

    def inc(self, instr, n=1):
        instr.then_inc(self.h, n)
        self.v += n
        return instr


class Chained:

    def __init__(self, eng, chsem, state, key):
        self._e = eng
        self._ch = chsem
        self._st = state
        self._k = key

    def wait_ge(self, h, v):
        return self._e.wait_ge(h, v)

    def __getattr__(self, name):
        f = getattr(self._e, name)

        def g(*a, _sem=None, **k):
            prev = self._st.get(self._k)
            if prev is not None:
                self._e.wait_ge(prev[0], prev[1])
            ins = f(*a, **k)
            sem = _sem if _sem is not None else self._ch
            sem.inc(ins)
            self._st[self._k] = (sem.h, sem.v)
            return ins

        return g


def build_nc(upto=None, dbg=None):
    nc = bass.Bass("TRN2", target_bir_lowering=False)

    def din(name, shape, dt=F32):
        return nc.dram_tensor(name, shape, dt, kind="ExternalInput").ap()

    x_sl = din("x_sl", [T, D])
    wqT = din("wqT", [D, 512])
    wkT = din("wkT", [D, 512])
    wvT = din("wvT", [D, 512])
    wosw = din("wosw", [16, 128, D])
    gsw = din("gsw", [IT, 128, D])
    usw = din("usw", [IT, 128, D])
    dsw = din("dsw", [HT, 128, INTER])
    gs_t = din("gs_t", [128, IT])
    us_t = din("us_t", [128, IT])
    ds_t = din("ds_t", [128, HT])
    aqT = din("aqT", [D, 64])
    akT = din("akT", [D, 64])
    avT = din("avT", [D, 64])
    aoT = din("aoT", [D, 64])
    bqT = din("bqT", [64, 512])
    bkT = din("bkT", [64, 512])
    bvT = din("bvT", [64, 512])
    boT = din("boT", [64, D])
    starts = din("starts", [4], I32)
    bsel = din("bsel", [1, 1])
    toff = din("toff", [1, 1], I32)
    cosT = din("cosT", [64, S])
    sinT = din("sinT", [64, S])
    masksI = din("masks", [4, 128, 512], BF16)
    identI = din("ident", [128, 128])
    s4I = din("s4", [64, 16])
    rep4I = din("rep4", [4, 64])
    c4I = din("c4row", [1, 4])

    y = nc.dram_tensor("y", [T, D], F32, kind="ExternalOutput").ap()

    cc_xn_in = nc.dram_tensor("cc_xn_in", [T, D], BF16)
    cc_xn_out = nc.dram_tensor("cc_xn_out", [S, D], BF16)
    cc_ai = [nc.dram_tensor(f"cc_ai{h}", [128, S], BF16) for h in range(NHL)]
    cc_ao = [nc.dram_tensor(f"cc_ao{h}", [512, S], BF16) for h in range(NHL)]
    hdn_d = nc.dram_tensor("hdn_d", [IT, 128, T], BF16)

    dbg_out = None
    if dbg is not None:
        dbg_shapes = {
            "xn": ([S, D], BF16), "xnl": ([T, D], BF16),
            "qk": ([128, 2 * NHL * S], BF16), "v": ([128, NHL * S], BF16),
            "low": ([80, S], BF16), "att": ([NHL * 512, S], BF16),
            "hdn": ([IT, 128, T], BF16), "gt": ([128, HT * T], F32),
            "sg": ([128, HT * T], F32),
        }
        shp, dt = dbg_shapes[dbg]
        dbg_out = nc.dram_tensor("dbg", shp, dt, kind="ExternalOutput").ap()

    RG = [[0, 1, 2, 3], [4, 5, 6, 7]]

    with ExitStack() as st:
        sems = {}
        for n in ("d_const d_cs d_cs1 d_x0 d_x1 d_x2 d_x3 d_xnw0 d_xnw1 d_xnsl d_xnt0 d_xnt1 "
                  "d_hdnw0 d_hdnw1 d_hdnr d_atw d_atr "
                  "d_wg0 d_wg1 d_wg2 d_wg3 d_wqkv d_wv d_wd0 d_wd1 d_wo0 d_wo1 d_xf0 d_xf1 "
                  "d_out cc0 cca0 cca1 cca2 cca3 "
                  "act_sq act_ln act_rs act_b act_p1 act_exp act_p4 "
                  "dve_st dve_vs dve_ln dve_b1 dve_b2 dve_b3 dve_b4 dve_boot "
                  "dve_p1 dve_p2 dve_low dve_p2b dve_msk dve_rl dve_att "
                  "dve_lo dve_p4 dve_p5 dve_p6 "
                  "pe_b pe_p1 pe_p2 pe_p2b pe_sc pe_av pe_avc pe_rb pe_p4 pe_tp dve_tp "
                  "pe_lo pe_p5 pe_p6 pool_ms ch_dve ch_act").split():
            sems[n] = Sem(nc, st, n)
        s = type("SS", (), sems)
        cca = [s.cca0, s.cca1, s.cca2, s.cca3]
        d_x = [s.d_x0, s.d_x1, s.d_x2, s.d_x3]
        d_xnw = [s.d_xnw0, s.d_xnw1]
        d_xnt = [s.d_xnt0, s.d_xnt1]
        d_hdnw = [s.d_hdnw0, s.d_hdnw1]
        d_wg = [s.d_wg0, s.d_wg1, s.d_wg2, s.d_wg3]
        d_wd = [s.d_wd0, s.d_wd1]
        d_wo = [s.d_wo0, s.d_wo1]
        d_xf = [s.d_xf0, s.d_xf1]
        wv_x = {}
        ch_state = {}

        def sb(name, shape, dt, **kw):
            return st.enter_context(nc.sbuf_tensor(name, shape, dt, **kw))

        big = st.enter_context(nc.sbuf_tensor("big_arena", [128, 94208], BF16,
                                              side="right"))

        def mt(name, shape, dt, addr, **kw):
            off = (addr - 40960) // 2
            cols_bf = shape[1] * ({F32: 4, BF16: 2, I32: 4}[dt] // 2)
            ap = big[0:shape[0], off:off + cols_bf]
            if dt != BF16:
                ap = ap.bitcast(dt)
            return ap

        A_Q, A_K, A_V = 40960, 73728, 106496
        A_HDN = 139264
        A_XT = 139264
        A_AT4 = 172032
        A_BT = 188416
        A_SQ = 204800
        A_XN = 212992
        A_XNSL = 94208
        A_GUB = 40960
        A_SGM = 73728
        A_T1B = 81920
        A_HSB = 86016
        A_XNT = 139264
        A_WV = 172032
        A_CST = 188416
        A_RT = 192512
        A_ATS = 196608
        A_PT = 204800
        A_MSK = 208896
        A_RL = 212992
        A_RB = 215040
        A_LO = 61440
        A_WQ2 = 196608
        A_WK2 = 212992
        A_DBUF = 40960
        A_GT = 63488
        A_DSG = 129024
        A_WO = 40960
        A_SG = 129024
        A_ATT = 194560
        A_XT2 = 49152
        A_OSB = 65536

        ident = sb("identS", [128, 128], F32)
        identb = sb("identb", [128, 128], BF16)
        ones128b = sb("ones128b", [128, 1], BF16)
        ones1x128 = sb("ones1x128", [1, 128], F32)
        gs_sb = sb("gs_sb", [128, IT], F32)
        us_sb = sb("us_sb", [128, IT], F32)
        ds_sb = sb("ds_sb", [128, HT], F32)
        c4row = sb("c4rowS", [1, 4], F32)
        st_i = sb("st_i", [1, 4], I32)
        st_f = sb("st_f", [1, 4], F32)
        bsel_sb = sb("bsel_sb", [1, 1], F32)
        toff_sb = sb("toff_sb", [1, 1], I32)
        s4_sb = sb("s4_sb", [64, 16], F32)
        rep4_sb = sb("rep4_sb", [4, 64], F32)
        u1 = sb("u1", [1, 4], F32)
        cnt = sb("cnt", [1, 1], F32)
        idv = sb("idv", [1, 1], F32)
        nidv = sb("nidv", [1, 1], F32)
        dtile = sb("dtile", [1, 4], F32)
        adtile = sb("adtile", [1, 4], F32)
        w4row = sb("w4row", [1, 4], F32)
        w4col = sb("w4col", [4, 1], F32)
        wcol = sb("wcol", [64, 1], F32)
        wb_sb = sb("wb_sb", [128, 4], F32)
        s4w = sb("s4w", [64, 16], F32)
        A_all = sb("A_all", [128, DC * 80], BF16)
        Aol = sb("Aol", [128, DC * RK], BF16)
        beff_pack = sb("beff_pack", [128, 512], BF16)
        beffo = sb("beffo", [16, D], BF16)
        low_all = sb("low_all", [128, S], BF16)
        cb_eps = sb("cb_eps", [128, 1], F32)
        cb_1 = sb("cb_1", [128, 1], F32)
        cb_m1 = sb("cb_m1", [128, 1], F32)

        blk = st.enter_context(nc.Block())
        toff_reg = st.enter_context(nc.sync.register("toffr"))
        toff_val = [None]

        xt = [mt(f"xt{i}", [128, D], F32, A_XT + 8192 * i) for i in range(4)]
        sq = mt("sqd", [128, D], F32, A_SQ)
        xn_sb = [mt(f"xn{i}", [128, D], BF16, A_XN + 4096 * i) for i in range(4)]
        xnsl = mt("xnsl", [128, DC * T], BF16, A_XNSL)
        xnsl3 = xnsl[:, :].rearrange("p (c e) -> p c e", e=T)
        msumW = sb("msumW", [128, 8], F32)
        vsumW = sb("vsumW", [128, 8], F32)
        nmeanW = sb("nmeanW", [128, 8], F32)
        m2W = sb("m2W", [128, 8], F32)
        veW = sb("veW", [128, 8], F32)
        lnvW = sb("lnvW", [128, 8], F32)
        rstdW = sb("rstdW", [128, 8], F32)
        aT_sb = {p: mt(f"aT_{p}", [128, DC * 64], F32, A_AT4 + 4096 * i)
                 for i, p in enumerate("qkvo")}
        bt_sb = {"q": mt("bt_q", [64, 512], F32, A_BT),
                 "k": mt("bt_k", [64, 512], F32, A_BT + 2048),
                 "v": mt("bt_v", [64, 512], F32, A_BT + 4096),
                 "o": mt("bt_o", [64, D], F32, A_BT + 6144)}
        selp = [sb(f"selp{i}", [128, 256], F32) for i in range(2)]
        p0ps = ExitStack()
        boot_ps = p0ps.enter_context(nc.psum_tensor("boot_ps", [128, 512], F32))
        tp_ps = [p0ps.enter_context(nc.psum_tensor(f"tp{i}", [128, 512], F32))
                 for i in range(2)]
        beff_ps = p0ps.enter_context(nc.psum_tensor("beff_ps", [16, D], F32))

        @blk.sync
        def _(sy):
            for tt in range(4):
                d_x[tt % 4].inc(sy.dma_start(out=xt[tt % 4][:, :],
                                             in_=x_sl[128 * tt:128 * (tt + 1), :]), 16)
                wv_x[("x", tt)] = d_x[tt % 4].v
            for dst, src in ((ident, identI), (gs_sb, gs_t), (us_sb, us_t),
                             (ds_sb, ds_t), (c4row, c4I), (bsel_sb, bsel),
                             (toff_sb, toff), (s4_sb, s4I), (rep4_sb, rep4I)):
                s.d_const.inc(sy.dma_start(out=dst[:, :], in_=src), 16)
            s.d_const.inc(sy.dma_start(
                out=st_i[:, :], in_=starts.rearrange("(one f) -> one f", one=1)), 16)
            for p, src in (("q", aqT), ("k", akT), ("v", avT), ("o", aoT)):
                s.d_const.inc(sy.dma_start(
                    out=aT_sb[p][:, :].rearrange("p (c r) -> p c r", r=64),
                    in_=src.rearrange("(c p) r -> p c r", p=128)), 16)
            for p, src in (("q", bqT), ("k", bkT), ("v", bvT), ("o", boT)):
                s.d_const.inc(sy.dma_start(out=bt_sb[p][:, :], in_=src), 16)
            sy.wait_ge(s.d_const.h, s.d_const.v)
            sy.reg_load(toff_reg, toff_sb[0:1, 0:1])
            toff_val[0] = sy.snap(toff_reg)
            for tt in range(8):
                sy.wait_ge(s.dve_ln.h, tt + 1)
                d_xnw[tt % 2].inc(sy.dma_start(
                    out=cc_xn_in.ap()[128 * tt:128 * (tt + 1), :],
                    in_=xn_sb[tt % 4][:, :]), 16)
                wv_x[("xnw", tt)] = d_xnw[tt % 2].v
                if tt + 4 < 8:
                    d_x[tt % 4].inc(sy.dma_start(
                        out=xt[tt % 4][:, :],
                        in_=x_sl[128 * (tt + 4):128 * (tt + 5), :]), 16)
                    wv_x[("x", tt + 4)] = d_x[tt % 4].v

        CONST_N = sems["d_const"].v

        @blk.vector
        def _(v):
            v = Chained(v, s.ch_dve, ch_state, 'v')

            def ln_dwave(w):
                c = slice(2 * w, 2 * w + 2)
                v.wait_ge(s.act_sq.h, w + 1)
                v.scalar_tensor_tensor(veW[:, c], vsumW[:, c], 1.0 / D, m2W[:, c],
                                       ALU.mult, ALU.subtract, _sem=s.dve_st)
                v.wait_ge(s.act_rs.h, w + 1)
                for tt in range(2 * w, 2 * w + 2):
                    if tt >= 4:
                        v.wait_ge(d_xnw[tt % 2].h, wv_x[("xnw", tt - 2)])
                        v.wait_ge(s.pe_tp.h, 4 * (tt - 4) + 4)
                    v.tensor_scalar(xn_sb[tt % 4][:, :], xt[tt % 4][:, :],
                                    nmeanW[:, tt:tt + 1], rstdW[:, tt:tt + 1],
                                    ALU.add, ALU.mult, _sem=s.dve_ln)
                for g in range(8 * w, 8 * w + 8):
                    tt, q = g // 4, g % 4
                    v.wait_ge(s.pe_tp.h, g + 1)
                    v.tensor_copy(xnsl3[:, 4 * q:4 * q + 4, 128 * tt:128 * (tt + 1)],
                                  tp_ps[g % 2][:, :].rearrange("p (c e) -> p c e", e=128),
                                  _sem=s.dve_tp)
            ln_dwave(0)
            ln_dwave(1)
            v.wait_ge(s.d_const.h, CONST_N)
            v.tensor_copy(st_f[:, :], st_i[:, :])
            v.tensor_scalar(u1[:, :], st_f[:, :], -1.0, bsel_sb[0:1, 0:1], ALU.mult, ALU.add)
            v.tensor_scalar(u1[:, :], u1[:, :], 0.0, 1.0, ALU.max, ALU.min)
            v.reduce_sum(cnt[:, :], u1[:, :], axis=mybir.AxisListType.X, _sem=s.dve_b1)
            v.wait_ge(s.act_b.h, 2)
            v.tensor_scalar_add(dtile[:, :], c4row[:, :], nidv[0:1, 0:1], _sem=s.dve_b2)
            v.wait_ge(s.pe_b.h, 1)
            v.tensor_copy(w4col[:, :], boot_ps[0:4, 0:1], _sem=s.dve_b3)
            v.wait_ge(s.pe_b.h, 2)
            v.tensor_copy(wcol[:, :], boot_ps[0:64, 4:5])
            v.tensor_copy(wb_sb[:, :], boot_ps[:, 8:12])
            v.tensor_scalar_mul(s4w[:, :], s4_sb[:, :], wcol[:, 0:1], _sem=s.dve_b4)
            v.wait_ge(s.pe_b.h, 3)
            v.tensor_copy(beff_pack[0:16, :], beff_ps[0:16, 0:512])
            v.tensor_copy(beff_pack[32:48, :], beff_ps[0:16, 512:1024])
            v.tensor_copy(beff_pack[64:80, :], beff_ps[0:16, 1024:1536], _sem=s.dve_b4)
            v.wait_ge(s.pe_b.h, 4)
            v.tensor_copy(beffo[:, :], beff_ps[0:16, :])
            ln_dwave(2)
            ln_dwave(3)
            v.wait_ge(s.pool_ms.h, 1)
            offs = {"q": 0, "k": 32, "v": 64}
            t03 = selp[0][:, :].rearrange("p (c e) -> p c e", e=16)
            t13 = selp[1][:, :].rearrange("p (c e) -> p c e", e=16)
            A3 = A_all[:, :].rearrange("p (c e) -> p c e", e=80)
            Ao3 = Aol[:, :].rearrange("p (c e) -> p c e", e=RK)
            for p in "qkvo":
                a3 = aT_sb[p][:, :].rearrange("p (c e) -> p c e", e=64)
                v.tensor_scalar_mul(t03, a3[:, :, 0:16], wb_sb[:, 0:1])
                v.scalar_tensor_tensor(t13, a3[:, :, 16:32], wb_sb[:, 1:2], t03,
                                       ALU.mult, ALU.add)
                v.scalar_tensor_tensor(t03, a3[:, :, 32:48], wb_sb[:, 2:3], t13,
                                       ALU.mult, ALU.add)
                dst = Ao3[:, :, :] if p == "o" else A3[:, :, offs[p]:offs[p] + 16]
                v.scalar_tensor_tensor(dst, a3[:, :, 48:64], wb_sb[:, 3:4], t03,
                                       ALU.mult, ALU.add,
                                       _sem=(s.dve_boot if p == "o" else None))

        @blk.scalar
        def _(sc):
            sc = Chained(sc, s.ch_act, ch_state, 'a')
            sc.wait_ge(s.pool_ms.h, 1)
            def ln_wave(w):
                for t2 in range(2):
                    tt = 2 * w + t2
                    sc.wait_ge(d_x[tt % 4].h, wv_x[("x", tt)])
                    sc.activation(sq[:, :], xt[tt % 4][:, :], AF.Copy, bias=0.0,
                                  scale=1.0, accum_out=msumW[:, tt:tt + 1])
                    sc.activation(sq[:, :], xt[tt % 4][:, :], AF.Square, bias=0.0,
                                  scale=1.0, accum_out=vsumW[:, tt:tt + 1])
                c = slice(2 * w, 2 * w + 2)
                sc.activation(nmeanW[:, c], msumW[:, c], AF.Copy, bias=0.0,
                              scale=-1.0 / D)
                sc.activation(m2W[:, c], nmeanW[:, c], AF.Square, bias=0.0,
                              scale=1.0, _sem=s.act_sq)
                sc.wait_ge(s.dve_st.h, w + 1)
                sc.activation(lnvW[:, c], veW[:, c], AF.Ln, bias=cb_eps[:, 0:1],
                              scale=1.0)
                sc.activation(rstdW[:, c], lnvW[:, c], AF.Exp, bias=0.0,
                              scale=-0.5, _sem=s.act_rs)
            ln_wave(0)
            ln_wave(1)
            sc.wait_ge(s.d_const.h, CONST_N)
            sc.wait_ge(s.dve_b1.h, 1)
            sc.activation(idv[:, :], cnt[:, :], AF.Relu, bias=cb_m1[0:1, 0:1], scale=1.0, _sem=s.act_b)
            sc.activation(nidv[:, :], idv[:, :], AF.Copy, bias=0.0, scale=-1.0, _sem=s.act_b)
            sc.wait_ge(s.dve_b2.h, 1)
            sc.activation(adtile[:, :], dtile[:, :], AF.Abs, bias=0.0, scale=1.0, _sem=s.act_b)
            sc.activation(w4row[:, :], adtile[:, :], AF.Relu, bias=cb_1[0:1, 0:1], scale=-1.0, _sem=s.act_b)
            ln_wave(2)
            ln_wave(3)

        def tp_tile(t, tt):
            t.wait_ge(s.dve_ln.h, tt + 1)
            for q in range(4):
                g = 4 * tt + q
                if g == 0:
                    t.wait_ge(s.act_ln.h, 16)
                if g >= 2:
                    t.wait_ge(s.dve_tp.h, g - 1)
                last = None
                for j in range(4):
                    dc = 4 * q + j
                    last = t.matmul(tp_ps[g % 2][:, 128 * j:128 * (j + 1)],
                                    xn_sb[tt % 4][:, 128 * dc:128 * (dc + 1)],
                                    identb[:, :], start=True, stop=True,
                                    skip_group_check=True)
                s.pe_tp.inc(last)

        @blk.tensor
        def _(t):
            for tt in range(4):
                tp_tile(t, tt)
            t.wait_ge(s.act_b.h, 4)
            s.pe_b.inc(t.matmul(boot_ps[0:4, 0:1], w4row[:, :], ones1x128[0:1, 0:1],
                                start=True, stop=True, skip_group_check=True))
            t.wait_ge(s.dve_b3.h, 1)
            t.matmul(boot_ps[0:64, 4:5], rep4_sb[:, :], w4col[:, :],
                     start=True, stop=True, skip_group_check=True)
            s.pe_b.inc(t.matmul(boot_ps[:, 8:12], ones1x128[:, :], w4row[:, :],
                                start=True, stop=True, skip_group_check=True))
            t.wait_ge(s.dve_b4.h, 1)
            t.matmul(beff_ps[0:16, 0:512], s4w[:, :], bt_sb["q"][:, :],
                     start=True, stop=True, skip_group_check=True)
            t.matmul(beff_ps[0:16, 512:1024], s4w[:, :], bt_sb["k"][:, :],
                     start=True, stop=True, skip_group_check=True)
            s.pe_b.inc(t.matmul(beff_ps[0:16, 1024:1536], s4w[:, :], bt_sb["v"][:, :],
                                start=True, stop=True, skip_group_check=True))
            t.wait_ge(s.dve_b4.h, 2)
            last = None
            for j in range(4):
                last = t.matmul(beff_ps[0:16, 512 * j:512 * (j + 1)], s4w[:, :],
                                bt_sb["o"][:, 512 * j:512 * (j + 1)],
                                start=True, stop=True, skip_group_check=True)
            s.pe_b.inc(last)
            for tt in range(4, 8):
                tp_tile(t, tt)

        @blk.gpsimd
        def _(g):
            s.act_ln.inc(g.dma_start(out=identb[:, :], in_=identI), 16)
            g.memset(ones128b[:, :], 1.0)
            g.memset(ones1x128[:, :], 1.0)
            g.memset(cb_eps[:, :], 1e-5)
            g.memset(cb_1[:, :], 1.0)
            g.memset(cb_m1[:, :], -1.0)
            g.memset(A_all[:, :], 0.0)
            s.pool_ms.inc(g.memset(Aol[:, :], 0.0))

        if dbg == "xnl":
            @blk.sync
            def _(sy):
                sy.wait_ge(d_xnw[0].h, 64)
                sy.wait_ge(d_xnw[1].h, 64)
                s.d_out.inc(sy.dma_start(out=dbg_out, in_=cc_xn_in.ap()[:, :]), 16)

        if upto == "p0":
            p0ps.close()
            _finish(nc, blk, s, y)
            return nc

        p0ps.close()
        xnt = [mt(f"xnt{i}", [128, DC * 512], BF16, A_XNT + 16384 * i) for i in range(2)]
        cst = [mt(f"cst{i}", [128, 512], F32, A_CST + 2048 * i) for i in range(2)]
        d_cs2 = [sems["d_cs"], sems["d_cs1"]]
        wv_sb = mt("wv_sb", [128, DC * 512], BF16, A_WV)
        wq_sb = mt("wq_sb", [128, DC * 512], BF16, A_WQ2)
        wk_sb = mt("wk_sb", [128, DC * 512], BF16, A_WK2)
        gbuf = [mt(f"gbuf{i}", [128, D], BF16, A_GUB + 4096 * i) for i in range(4)]
        pk_g = {it: mt(f"pkg{it}", [128, D], BF16, 172032 + 8192 * (it - 4))
                for it in range(4, 11)}
        pk_u = {it: mt(f"pku{it}", [128, D], BF16, 172032 + 8192 * (it - 4) + 4096)
                for it in range(4, 11)}
        pk_sem = {4: s.d_wg0, 5: s.d_wg1, 6: s.d_wg2, 7: s.d_wg3,
                  8: s.d_xf0, 9: s.d_xf1, 10: s.d_out}
        ubuf = [mt(f"ubuf{i}", [128, D], BF16, A_GUB + 16384 + 4096 * i) for i in range(4)]
        sgm = [mt(f"sgm{i}", [128, T], F32, A_SGM + 4096 * i) for i in range(2)]
        t1b = mt("t1b", [128, T], F32, A_T1B)
        hdn_sb = [mt(f"hdnsb{i}", [128, T], BF16, A_HSB + 2048 * i) for i in range(4)]
        p1 = ExitStack()
        gp_ps = [p1.enter_context(nc.psum_tensor(f"gp{i}", [128, T], F32)) for i in range(2)]
        up_ps = [p1.enter_context(nc.psum_tensor(f"up{i}", [128, T], F32)) for i in range(2)]

        @blk.gpsimd
        def _(g):
            for it in range(4):
                d_wg[it % 4].inc(g.dma_start(out=gbuf[it % 4][:, :], in_=gsw[it, :, :]), 16)
                d_wg[it % 4].inc(g.dma_start(out=ubuf[it % 4][:, :], in_=usw[it, :, :]), 16)
                wv_x[("wg", it)] = d_wg[it % 4].v
            g.wait_ge(d_xnw[0].h, 64)
            g.wait_ge(d_xnw[1].h, 64)
            s.cc0.inc(g.collective_compute(
                "AllGather", ALU.bypass, replica_groups=RG,
                ins=[cc_xn_in.ap()], outs=[cc_xn_out.ap()]))
            g.wait_ge(s.dve_boot.h, 1)
            for it in range(4, 11):
                sem = pk_sem[it]
                sem.inc(g.dma_start(out=pk_g[it][:, :], in_=gsw[it, :, :]), 16)
                sem.inc(g.dma_start(out=pk_u[it][:, :], in_=usw[it, :, :]), 16)
                wv_x[("park", it)] = sem.v
            for it in range(11, IT):
                g.wait_ge(s.pe_p1.h, it - 3)
                d_wg[it % 4].inc(g.dma_start(out=gbuf[it % 4][:, :], in_=gsw[it, :, :]), 16)
                d_wg[it % 4].inc(g.dma_start(out=ubuf[it % 4][:, :], in_=usw[it, :, :]), 16)
                wv_x[("wg", it)] = d_wg[it % 4].v
                if 16 <= it < 40:
                    qi = it - 16
                    wsb, wsrc, sem = ((wq_sb, wqT, s.d_wqkv), (wk_sb, wkT, s.dve_vs),
                                      (wv_sb, wvT, s.d_wv))[qi // 8]
                    q = qi % 8
                    sem.inc(g.dma_start(
                        out=wsb[:, 1024 * q:1024 * (q + 1)].rearrange(
                            "p (c f) -> p c f", f=512),
                        in_=wsrc[256 * q:256 * (q + 1), :].rearrange(
                            "(c p) f -> p c f", p=128)), 16)

        @blk.tensor
        def _(t):
            t.wait_ge(s.dve_boot.h, 1)
            t.wait_ge(s.dve_tp.h, 32)
            for it in range(IT):
                if 4 <= it <= 10:
                    t.wait_ge(pk_sem[it].h, wv_x[("park", it)])
                else:
                    t.wait_ge(d_wg[it % 4].h, wv_x[("wg", it)])
                if it >= 2:
                    t.wait_ge(s.act_p1.h, it - 1)
                    t.wait_ge(s.dve_p1.h, it - 1)
                b = it % 2
                last = None
                gt_, ut_ = ((pk_g[it], pk_u[it]) if 4 <= it <= 10
                            else (gbuf[it % 4], ubuf[it % 4]))
                for dc in range(DC):
                    gl = gt_[:, 128 * dc:128 * (dc + 1)]
                    ul = ut_[:, 128 * dc:128 * (dc + 1)]
                    for th in range(2):
                        rh = xnsl[:, T * dc + 512 * th:T * dc + 512 * (th + 1)]
                        t.matmul(gp_ps[b][:, 512 * th:512 * (th + 1)], gl, rh,
                                 start=(dc == 0), stop=(dc == DC - 1), skip_group_check=True)
                        last = t.matmul(up_ps[b][:, 512 * th:512 * (th + 1)], ul, rh,
                                        start=(dc == 0), stop=(dc == DC - 1),
                                        skip_group_check=True)
                s.pe_p1.inc(last)

        @blk.scalar
        def _(sc):
            sc = Chained(sc, s.ch_act, ch_state, 'a')
            for it in range(IT):
                b = it % 2
                sc.wait_ge(s.pe_p1.h, it + 1)
                if it >= 2:
                    sc.wait_ge(s.dve_p1.h, it - 1)
                sc.activation(sgm[b][:, :], gp_ps[b][:, :], AF.Sigmoid,
                              bias=0.0, scale=gs_sb[:, it:it + 1], _sem=s.act_p1)

        @blk.sync
        def _(sy):
            for it in range(IT):
                sy.wait_ge(s.dve_p1.h, it + 1)
                d_hdnw[it % 2].inc(sy.dma_start(out=hdn_d.ap()[it, :, :],
                                                in_=hdn_sb[it % 4][:, :]), 16)
                wv_x[("hdnw", it)] = d_hdnw[it % 2].v
                if 24 <= it < 32:
                    k = it - 24
                    tc, q4 = k // 4, k % 4
                    if k == 0:
                        sy.wait_ge(s.cc0.h, 1)
                    if q4 == 0:
                        d_cs2[tc].inc(sy.dma_start(
                            out=cst[tc][0:64, :],
                            in_=cosT[:, 512 * tc:512 * (tc + 1)]), 16)
                        d_cs2[tc].inc(sy.dma_start(
                            out=cst[tc][64:128, :],
                            in_=sinT[:, 512 * tc:512 * (tc + 1)]), 16)
                        wv_x[("cs", tc)] = d_cs2[tc].v
                    for dc in range(4 * q4, 4 * q4 + 4):
                        d_xnt[tc].inc(sy.dma_start(
                            out=xnt[tc][:, 512 * dc:512 * (dc + 1)],
                            in_=cc_xn_out.ap()[512 * tc:512 * (tc + 1),
                                               128 * dc:128 * (dc + 1)],
                            transpose=True), 16)
                    if q4 == 3:
                        wv_x[("xnt", tc)] = d_xnt[tc].v

        @blk.vector
        def _(v):
            v = Chained(v, s.ch_dve, ch_state, 'v')
            for it in range(IT):
                b = it % 2
                v.wait_ge(s.pe_p1.h, it + 1)
                v.wait_ge(s.act_p1.h, it + 1)
                if it >= 4:
                    v.wait_ge(d_hdnw[it % 2].h, wv_x[("hdnw", it - 2)])
                v.scalar_tensor_tensor(t1b[:, :], gp_ps[b][:, :], gs_sb[:, it:it + 1],
                                       sgm[b][:, :], ALU.mult, ALU.mult)
                v.scalar_tensor_tensor(
                    hdn_sb[it % 4][:, :], up_ps[b][:, :], us_sb[:, it:it + 1],
                    t1b[:, :], ALU.mult, ALU.mult, _sem=s.dve_p1)

        if dbg == "hdn":
            @blk.sync
            def _(sy):
                sy.wait_ge(d_hdnw[0].h, 16 * (IT // 2))
                sy.wait_ge(d_hdnw[1].h, 16 * (IT // 2))
                s.d_out.inc(sy.dma_start(out=dbg_out, in_=hdn_d.ap()[:, :, :]), 16)
        if dbg == "xn":
            @blk.sync
            def _(sy):
                sy.wait_ge(s.cc0.h, 1)
                s.d_out.inc(sy.dma_start(out=dbg_out, in_=cc_xn_out.ap()[:, :]), 16)

        p1.close()

        if upto == "p1":
            _finish(nc, blk, s, y)
            return nc

        rta = mt("rta", [64, 512], F32, A_RT)
        rtb = mt("rtb", [64, 512], F32, A_RT + 2048)
        Q_sb = mt("Q_sb", [128, NHL * S], BF16, A_Q)
        K_sb = mt("K_sb", [128, NHL * S], BF16, A_K)
        V_sb = mt("V_sb", [128, NHL * S], BF16, A_V)
        p2 = ExitStack()
        qk_ps = [p2.enter_context(nc.psum_tensor(f"qk{i}", [128, 512], F32)) for i in range(4)]
        NG = 13

        @blk.sync
        def _(sy):
            sy.wait_ge(s.pe_p1.h, IT)
            sy.wait_ge(s.dve_p1.h, IT)
            sy.wait_ge(d_hdnw[0].h, 16 * (IT // 2))
            sy.wait_ge(d_hdnw[1].h, 16 * (IT // 2))
            sy.wait_ge(s.cc0.h, 1)
            for tc in range(2, 8):
                sy.wait_ge(s.pe_p2.h, NG * (tc - 1))
                sy.wait_ge(s.dve_p2.h, NG * (tc - 1))
                d_cs2[tc % 2].inc(sy.dma_start(
                    out=cst[tc % 2][0:64, :], in_=cosT[:, 512 * tc:512 * (tc + 1)]), 16)
                d_cs2[tc % 2].inc(sy.dma_start(
                    out=cst[tc % 2][64:128, :], in_=sinT[:, 512 * tc:512 * (tc + 1)]), 16)
                wv_x[("cs", tc)] = d_cs2[tc % 2].v
                for dc in range(DC):
                    d_xnt[tc % 2].inc(sy.dma_start(
                        out=xnt[tc % 2][:, 512 * dc:512 * (dc + 1)],
                        in_=cc_xn_out.ap()[512 * tc:512 * (tc + 1), 128 * dc:128 * (dc + 1)],
                        transpose=True), 16)
                wv_x[("xnt", tc)] = d_xnt[tc % 2].v

        @blk.tensor
        def _(t):
            fi = 0
            for tc in range(8):
                t.wait_ge(d_xnt[tc % 2].h, wv_x[("xnt", tc)])
                for f in range(NG):
                    ps = qk_ps[fi % 4]
                    if fi >= 4:
                        t.wait_ge(s.dve_p2.h, fi - 3)
                    if f == 0:
                        last = None
                        for dc in range(DC):
                            last = t.matmul(ps[0:80, :], A_all[:, 80 * dc:80 * (dc + 1)],
                                            xnt[tc % 2][:, 512 * dc:512 * (dc + 1)],
                                            start=(dc == 0), stop=(dc == DC - 1),
                                            skip_group_check=True)
                        s.pe_p2.inc(last)
                    elif f <= 8:
                        qk = 0 if f <= 4 else 1
                        h = (f - 1) % 4
                        if fi == 1:
                            t.wait_ge(s.d_wqkv.h, 128)
                        if fi == 5:
                            t.wait_ge(s.dve_vs.h, 128)
                        wsb = wq_sb if qk == 0 else wk_sb
                        for dc in range(DC):
                            t.matmul(ps[:, :],
                                     wsb[:, 512 * dc + 128 * h:512 * dc + 128 * (h + 1)],
                                     xnt[tc % 2][:, 512 * dc:512 * (dc + 1)],
                                     start=(dc == 0), stop=False, skip_group_check=True)
                        t.wait_ge(s.dve_p2.h, NG * tc + 1)
                        pb = 0 if qk == 0 else 32
                        s.pe_p2.inc(t.matmul(
                            ps[:, :], beff_pack[pb:pb + 16, 128 * h:128 * (h + 1)],
                            low_all[pb:pb + 16, 512 * tc:512 * (tc + 1)],
                            start=False, stop=True, skip_group_check=True))
                    else:
                        j = f - 9
                        if fi == 9:
                            t.wait_ge(s.d_wv.h, 128)
                        for dc in range(DC):
                            t.matmul(ps[:, :],
                                     xnt[tc % 2][:, 512 * dc + 128 * j:512 * dc + 128 * (j + 1)],
                                     wv_sb[:, 512 * dc:512 * (dc + 1)],
                                     start=(dc == 0), stop=False, skip_group_check=True)
                        t.wait_ge(s.dve_p2.h, NG * tc + 1)
                        s.pe_p2.inc(t.matmul(
                            ps[:, :], low_all[64:80, 512 * tc + 128 * j:512 * tc + 128 * (j + 1)],
                            beff_pack[64:80, :], start=False, stop=True, skip_group_check=True))
                    fi += 1

        @blk.vector
        def _(v):
            v = Chained(v, s.ch_dve, ch_state, 'v')
            fi = 0
            for tc in range(8):
                v.wait_ge(d_cs2[tc % 2].h, wv_x[("cs", tc)])
                for f in range(NG):
                    ps = qk_ps[fi % 4]
                    v.wait_ge(s.pe_p2.h, fi + 1)
                    if f == 0:
                        v.tensor_copy(low_all[0:80, 512 * tc:512 * (tc + 1)],
                                      ps[0:80, :], _sem=s.dve_p2)
                    elif f <= 8:
                        qk = 0 if f <= 4 else 1
                        h = (f - 1) % 4
                        mat = Q_sb if qk == 0 else K_sb
                        col = S * h + 512 * tc
                        cs = cst[tc % 2][0:64, :]
                        sn = cst[tc % 2][64:128, :]
                        v.tensor_tensor(rta[:, :], ps[0:64, :], cs, ALU.mult)
                        v.tensor_tensor(rtb[:, :], ps[64:128, :], sn, ALU.mult)
                        v.tensor_tensor(mat[0:64, col:col + 512], rta[:, :], rtb[:, :],
                                        ALU.subtract)
                        v.tensor_tensor(rta[:, :], ps[0:64, :], sn, ALU.mult)
                        v.tensor_tensor(rtb[:, :], ps[64:128, :], cs, ALU.mult)
                        v.tensor_tensor(mat[64:128, col:col + 512],
                                        rta[:, :], rtb[:, :], ALU.add, _sem=s.dve_p2)
                    else:
                        j = f - 9
                        col = 128 * (4 * tc + j)
                        for h in range(NHL):
                            v.tensor_copy(V_sb[:, S * h + col:S * h + col + 128],
                                          ps[:, 128 * h:128 * (h + 1)],
                                          _sem=(s.dve_p2 if h == NHL - 1 else None))
                    fi += 1

        P2_TOT = 8 * NG
        p2.close()
        hdn_all = mt("hdn_all", [128, IT * T], BF16, A_HDN)

        @blk.sync
        def _(sy):
            sy.wait_ge(s.pe_p2.h, P2_TOT)
            sy.wait_ge(s.dve_p2.h, P2_TOT)
            for it in list(range(0, 28)) + list(range(38, IT)):
                s.d_hdnr.inc(sy.dma_start(out=hdn_all[:, T * it:T * (it + 1)],
                                          in_=hdn_d.ap()[it, :, :]), 16)

        if dbg == "qk":
            @blk.sync
            def _(sy):
                sy.wait_ge(s.dve_p2.h, P2_TOT)
                s.d_out.inc(sy.dma_start(out=dbg_out[:, 0:NHL * S], in_=Q_sb[:, :]), 16)
                s.d_out.inc(sy.dma_start(out=dbg_out[:, NHL * S:2 * NHL * S], in_=K_sb[:, :]), 16)
        if dbg == "v":
            @blk.sync
            def _(sy):
                sy.wait_ge(s.dve_p2.h, P2_TOT)
                s.d_out.inc(sy.dma_start(out=dbg_out, in_=V_sb[:, :]), 16)
        if dbg == "low":
            @blk.sync
            def _(sy):
                sy.wait_ge(s.dve_p2.h, NG * 7 + 1)
                s.d_out.inc(sy.dma_start(out=dbg_out[0:80, :], in_=low_all[0:80, :]), 16)
        if upto == "p2":
            _finish(nc, blk, s, y)
            return nc

        p3 = ExitStack()
        pt_sb = [mt(f"pt{i}", [128, 512], BF16, A_PT + 1024 * i) for i in range(4)]
        masks = mt("masksS", [128, 4 * 512], BF16, A_MSK)
        at_sb = mt("at_sb", [128, S], BF16, A_ATS)
        rl_sb = mt("rl_sb", [1, 512], F32, A_RL)
        rb_sb = mt("rb_sb", [128, 512], F32, A_RB)
        sc_ps = [p3.enter_context(nc.psum_tensor(f"scps{i}", [128, 512], F32)) for i in range(4)]
        av_ps = [p3.enter_context(nc.psum_tensor(f"avps{i}", [128, 512], F32)) for i in range(2)]
        l_ps = p3.enter_context(nc.psum_tensor("l_ps", [1, 512], F32))
        rb_ps = p3.enter_context(nc.psum_tensor("rb_ps", [128, 512], F32))

        dbuf = [mt(f"dbuf{i}", [128, INTER], BF16, A_DBUF + 11264 * i) for i in range(2)]
        sc_cnt = {}
        avc_cnt = {}
        msk_cnt = {}
        CH = 0
        AVC = 0
        MSK = 0
        chpar = {}
        for h in range(NHL):
            for qb in range(8):
                nch = 4 * qb + 4
                for c in range(nch):
                    CH += 1
                    sc_cnt[(h, qb, c)] = CH
                    chpar[(h, qb, c)] = (CH - 1) % 4
                    AVC += 1
                    avc_cnt[(h, qb, c)] = AVC
                    if c >= 4 * qb:
                        MSK += 1
                    msk_cnt[(h, qb, c)] = MSK
        SC_TOT = CH

        @blk.sync
        def _(sy):
            sy.wait_ge(s.dve_p2.h, P2_TOT)
            s.d_const.inc(sy.dma_start(
                out=masks[:, :].rearrange("p (m f) -> p m f", f=512),
                in_=masksI.rearrange("m p f -> p m f")), 16)

        @blk.tensor
        def _(t):
            t.wait_ge(s.dve_p2.h, P2_TOT)
            QBc = 0
            rb_pending = [None]
            for h in range(NHL):
                for qb in range(8):
                    nch = 4 * qb + 4
                    qcol = S * h + 512 * qb

                    def av_chunk(c, QBc=QBc, h=h, qb=qb, nch=nch):
                        if c == 0:
                            t.wait_ge(s.dve_rl.h, QBc)
                            if QBc >= 2:
                                t.wait_ge(s.dve_att.h, QBc - 1)
                        t.wait_ge(s.act_exp.h, sc_cnt[(h, qb, c)])
                        if c >= 4 * qb:
                            t.wait_ge(s.dve_msk.h, msk_cnt[(h, qb, c)])
                        pt = pt_sb[chpar[(h, qb, c)]][:, :]
                        t.matmul(av_ps[QBc % 2][:, :],
                                 V_sb[:, S * h + 128 * c:S * h + 128 * (c + 1)],
                                 pt, start=(c == 0), stop=(c == nch - 1),
                                 skip_group_check=True)
                        last = t.matmul(l_ps[:, :], ones128b[:, :], pt,
                                        start=(c == 0), stop=(c == nch - 1),
                                        skip_group_check=True)
                        s.pe_avc.inc(last)

                    for c in range(nch):
                        gch = sc_cnt[(h, qb, c)]
                        if gch >= 5:
                            t.wait_ge(s.act_exp.h, gch - 4)
                        last = t.matmul(
                            sc_ps[chpar[(h, qb, c)]][:, :],
                            K_sb[:, S * h + 128 * c:S * h + 128 * (c + 1)],
                            Q_sb[:, qcol:qcol + 512],
                            start=True, stop=True, skip_group_check=True)
                        s.pe_sc.inc(last)
                        if c == 1 and rb_pending[0] is not None:
                            t.wait_ge(s.dve_rl.h, rb_pending[0] + 1)
                            if rb_pending[0] >= 1:
                                t.wait_ge(s.dve_att.h, rb_pending[0])
                            s.pe_rb.inc(t.matmul(rb_ps[:, :], ones1x128[:, :],
                                                 rl_sb[:, :], start=True, stop=True,
                                                 skip_group_check=True))
                            rb_pending[0] = None
                        if c >= 2:
                            av_chunk(c - 2)
                    av_chunk(nch - 2)
                    av_chunk(nch - 1)
                    rb_pending[0] = QBc
                    QBc += 1
            t.wait_ge(s.dve_rl.h, rb_pending[0] + 1)
            t.wait_ge(s.dve_att.h, rb_pending[0])
            s.pe_rb.inc(t.matmul(rb_ps[:, :], ones1x128[:, :], rl_sb[:, :],
                                 start=True, stop=True, skip_group_check=True))

        @blk.scalar
        def _(sc):
            sc = Chained(sc, s.ch_act, ch_state, 'a')
            for h in range(NHL):
                for qb in range(8):
                    nch = 4 * qb + 4
                    for c in range(nch):
                        gch = sc_cnt[(h, qb, c)]
                        sc.wait_ge(s.pe_sc.h, gch)
                        if gch >= 5:
                            sc.wait_ge(s.pe_avc.h, gch - 4)
                        b = chpar[(h, qb, c)]
                        sc.activation(pt_sb[b][:, :], sc_ps[b][:, :], AF.Exp,
                                      bias=0.0, scale=SCALE, _sem=s.act_exp)

        @blk.vector
        def _(v):
            v = Chained(v, s.ch_dve, ch_state, 'v')
            v.wait_ge(s.d_const.h, s.d_const.v)
            QBc = 0
            for h in range(NHL):
                for qb in range(8):
                    nch = 4 * qb + 4
                    for c in range(4 * qb, nch):
                        gch = sc_cnt[(h, qb, c)]
                        m = c - 4 * qb
                        v.wait_ge(s.act_exp.h, gch)
                        b = chpar[(h, qb, c)]
                        v.tensor_tensor(pt_sb[b][:, :], pt_sb[b][:, :],
                                        masks[:, 512 * m:512 * (m + 1)],
                                        ALU.mult, _sem=s.dve_msk)
                    v.wait_ge(s.pe_avc.h, avc_cnt[(h, qb, nch - 1)])
                    v.reciprocal(rl_sb[:, :], l_ps[:, :], _sem=s.dve_rl)
                    v.wait_ge(s.pe_rb.h, QBc + 1)
                    v.tensor_copy(rb_sb[:, :], rb_ps[:, :])
                    if h >= 1:
                        v.wait_ge(s.d_atw.h, 16 * (8 * (h - 1) + qb + 1))
                    v.tensor_tensor(
                        at_sb[:, 512 * qb:512 * (qb + 1)], av_ps[QBc % 2][:, :],
                        rb_sb[:, :], ALU.mult, _sem=s.dve_att)
                    QBc += 1

        @blk.sync
        def _(sy):
            for h in range(NHL):
                for qb in range(8):
                    sy.wait_ge(s.dve_att.h, 8 * h + qb + 1)
                    s.d_atw.inc(sy.dma_start(
                        out=cc_ai[h].ap()[:, 512 * qb:512 * (qb + 1)],
                        in_=at_sb[:, 512 * qb:512 * (qb + 1)]), 16)
                    sy.wait_ge(s.d_atw.h, s.d_atw.v)
            for it in range(28, 38):
                s.d_hdnr.inc(sy.dma_start(out=hdn_all[:, T * it:T * (it + 1)],
                                          in_=hdn_d.ap()[it, :, :]), 16)

        @blk.gpsimd
        def _(g):
            for h in range(NHL):
                g.wait_ge(s.d_atw.h, 16 * 8 * (h + 1))
                cca[h].inc(g.collective_compute(
                    "AllGather", ALU.bypass, replica_groups=RG,
                    ins=[cc_ai[h].ap()], outs=[cc_ao[h].ap()]))
                if h in (1, 2):
                    g.wait_ge(s.pe_sc.h, 144 * (h + 1))
                    d_wd[h - 1].inc(g.dma_start(out=dbuf[h - 1][:, :],
                                                in_=dsw[h - 1, :, :]), 16)
                    wv_x[("wd", h - 1)] = d_wd[h - 1].v

        p3.close()

        if dbg == "att":
            @blk.sync
            def _(sy):
                for h in range(NHL):
                    sy.wait_ge(cca[h].h, 1)
                    s.d_out.inc(sy.dma_start(out=dbg_out[512 * h:512 * (h + 1), :],
                                             in_=cc_ao[h].ap()[:, :]), 16)
        if upto == "p3":
            _finish(nc, blk, s, y)
            return nc

        gT_sb = mt("gT_sb", [128, HT * T], F32, A_GT)
        dsg = [mt(f"dsg{i}", [128, T], F32, A_DSG + 4096 * i) for i in range(2)]
        p4 = ExitStack()
        g_ps = [p4.enter_context(nc.psum_tensor(f"gps{i}", [128, T], F32)) for i in range(2)]

        @blk.gpsimd
        def _(g):
            for ht in range(2, HT):
                g.wait_ge(s.pe_p4.h, ht - 1)
                d_wd[ht % 2].inc(g.dma_start(out=dbuf[ht % 2][:, :], in_=dsw[ht, :, :]), 16)
                wv_x[("wd", ht)] = d_wd[ht % 2].v

        @blk.tensor
        def _(t):
            t.wait_ge(s.act_exp.h, SC_TOT)
            t.wait_ge(s.dve_att.h, 32)
            t.wait_ge(s.d_hdnr.h, 16 * 34)
            C_ORDER = list(range(0, 28)) + list(range(38, IT)) + list(range(28, 38))
            for ht in range(HT):
                t.wait_ge(d_wd[ht % 2].h, wv_x[("wd", ht)])
                if ht >= 2:
                    t.wait_ge(s.act_p4.h, ht - 1)
                    t.wait_ge(s.dve_p4.h, ht - 1)
                b = ht % 2
                last = None
                co = (C_ORDER if ht != HT - 1 else
                      list(range(28, 38)) + list(range(0, 28)) + list(range(38, IT)))
                for ci, c in enumerate(co):
                    if ht == 0 and ci == 34:
                        t.wait_ge(s.d_hdnr.h, 16 * IT)
                    dl = dbuf[b][:, 128 * c:128 * (c + 1)]
                    for th in range(2):
                        last = t.matmul(
                            g_ps[b][:, 512 * th:512 * th + 512], dl,
                            hdn_all[:, T * c + 512 * th:T * c + 512 * th + 512],
                            start=(ci == 0), stop=(ci == IT - 1), skip_group_check=True)
                    if ht == HT - 1 and ci == 9:
                        s.pe_sc.inc(last)
                s.pe_p4.inc(last)

        @blk.scalar
        def _(sc):
            sc = Chained(sc, s.ch_act, ch_state, 'a')
            for ht in range(HT):
                b = ht % 2
                sc.wait_ge(s.pe_p4.h, ht + 1)
                if ht >= 2:
                    sc.wait_ge(s.dve_p4.h, ht - 1)
                sc.activation(dsg[b][:, :], g_ps[b][:, :], AF.Sigmoid,
                              bias=0.0, scale=ds_sb[:, ht:ht + 1], _sem=s.act_p4)

        @blk.vector
        def _(v):
            v = Chained(v, s.ch_dve, ch_state, 'v')
            for ht in range(HT):
                b = ht % 2
                v.wait_ge(s.act_p4.h, ht + 1)
                v.scalar_tensor_tensor(
                    gT_sb[:, T * ht:T * (ht + 1)], g_ps[b][:, :],
                    ds_sb[:, ht:ht + 1], dsg[b][:, :], ALU.mult, ALU.mult, _sem=s.dve_p4)

        p4.close()

        if dbg == "gt":
            @blk.sync
            def _(sy):
                sy.wait_ge(s.dve_p4.h, HT)
                s.d_out.inc(sy.dma_start(out=dbg_out, in_=gT_sb[:, :]), 16)
        if upto == "p4":
            _finish(nc, blk, s, y)
            return nc

        att_sb = mt("att_sb", [128, 16 * T], BF16, A_ATT)
        low_o = mt("low_o", [16, T], BF16, A_LO)
        wo_buf = [mt(f"wo{i}", [128, D], BF16, A_WO + 4096 * i) for i in range(2)]
        sg_sb = mt("sg_sb", [128, HT * T], F32, A_SG)
        p5 = ExitStack()
        lo_ps = p5.enter_context(nc.psum_tensor("lo_ps", [16, T], F32))
        s_ps = [p5.enter_context(nc.psum_tensor(f"sps{i}", [128, 512], F32)) for i in range(2)]

        @blk.sync
        def _(sy):
            sy.wait_ge(s.pe_sc.h, SC_TOT + 1)
            for H in list(range(1, 11)) + [0] + list(range(11, 16)):
                if H == 0:
                    sy.wait_ge(s.pe_p4.h, HT)
                hl, rho = H % 4, H // 4
                sy.wait_ge(cca[hl].h, 1)
                s.d_atr.inc(sy.dma_start(
                    out=att_sb[:, T * H:T * (H + 1)],
                    in_=cc_ao[hl].ap()[128 * rho:128 * (rho + 1), bass.ds(toff_val[0], T)]), 16)

        @blk.gpsimd
        def _(g):
            for oc in range(16):
                if oc < 2:
                    g.wait_ge(s.pe_p4.h, HT)
                else:
                    g.wait_ge(s.pe_p5.h, 2 * (oc - 1))
                d_wo[oc % 2].inc(g.dma_start(out=wo_buf[oc % 2][:, :], in_=wosw[oc, :, :]), 16)
                wv_x[("wo", oc)] = d_wo[oc % 2].v

        @blk.tensor
        def _(t):
            t.wait_ge(s.act_p4.h, HT)
            t.wait_ge(s.dve_p4.h, HT)
            t.wait_ge(s.d_atr.h, 16 * 16)
            last = None
            for th in range(2):
                for j in range(16):
                    last = t.matmul(
                        lo_ps[:, 512 * th:512 * th + 512], Aol[:, RK * j:RK * (j + 1)],
                        att_sb[:, T * j + 512 * th:T * j + 512 * th + 512],
                        start=(j == 0), stop=(j == 15), skip_group_check=True)
            s.pe_lo.inc(last)
            t.wait_ge(s.dve_lo.h, 1)
            pi = 0
            for oc in range(16):
                t.wait_ge(d_wo[oc % 2].h, wv_x[("wo", oc)])
                for th in range(2):
                    ps = s_ps[pi % 2]
                    if pi >= 2:
                        t.wait_ge(s.dve_p5.h, pi - 1)
                    for j in range(16):
                        t.matmul(ps[:, :], wo_buf[oc % 2][:, 128 * j:128 * (j + 1)],
                                 att_sb[:, T * j + 512 * th:T * j + 512 * th + 512],
                                 start=(j == 0), stop=False, skip_group_check=True)
                    s.pe_p5.inc(t.matmul(ps[:, :], beffo[:, 128 * oc:128 * (oc + 1)],
                                         low_o[:, 512 * th:512 * th + 512],
                                         start=False, stop=True, skip_group_check=True))
                    pi += 1

        @blk.vector
        def _(v):
            v = Chained(v, s.ch_dve, ch_state, 'v')
            v.wait_ge(s.pe_lo.h, 1)
            v.tensor_copy(low_o[:, :], lo_ps[:, :], _sem=s.dve_lo)
            pi = 0
            for oc in range(16):
                for th in range(2):
                    ps = s_ps[pi % 2]
                    v.wait_ge(s.pe_p5.h, pi + 1)
                    v.tensor_tensor(
                        sg_sb[:, T * oc + 512 * th:T * oc + 512 * th + 512], ps[:, :],
                        gT_sb[:, T * oc + 512 * th:T * oc + 512 * th + 512], ALU.mult,
                        _sem=s.dve_p5)
                    pi += 1

        p5.close()

        if dbg == "sg":
            @blk.sync
            def _(sy):
                sy.wait_ge(s.dve_p5.h, 32)
                s.d_out.inc(sy.dma_start(out=dbg_out, in_=sg_sb[:, :]), 16)
        if upto == "p5":
            _finish(nc, blk, s, y)
            return nc

        xt2 = [mt(f"xt2_{i}", [128, D], F32, A_XT2 + 8192 * i) for i in range(2)]
        out_sb = [mt(f"osb{i}", [128, D], F32, A_OSB + 8192 * i) for i in range(2)]
        p6 = ExitStack()
        tp_ps = [p6.enter_context(nc.psum_tensor(f"tpps{i}", [128, D], F32)) for i in range(2)]

        OUT0 = sems["d_out"].v

        d_yw = [sems["d_out"], sems["d_atw"]]

        @blk.sync
        def _(sy):
            sy.wait_ge(s.pe_p4.h, HT)
            d_xf[0].inc(sy.dma_start(out=xt2[0][:, :], in_=x_sl[0:128, :]), 16)
            wv_x[("xf", 0)] = d_xf[0].v
            sy.wait_ge(s.pe_p5.h, 32)
            d_xf[1].inc(sy.dma_start(out=xt2[1][:, :], in_=x_sl[128:256, :]), 16)
            wv_x[("xf", 1)] = d_xf[1].v
            for tt in range(8):
                sy.wait_ge(s.dve_p6.h, tt + 1)
                d_yw[tt % 2].inc(sy.dma_start(out=y[128 * tt:128 * (tt + 1), :],
                                              in_=out_sb[tt % 2][:, :]), 16)
                wv_x[("yw", tt)] = d_yw[tt % 2].v
                if tt + 2 < 8:
                    d_xf[tt % 2].inc(sy.dma_start(
                        out=xt2[tt % 2][:, :],
                        in_=x_sl[128 * (tt + 2):128 * (tt + 3), :]), 16)
                    wv_x[("xf", tt + 2)] = d_xf[tt % 2].v
            sy.wait_ge(s.d_out.h, s.d_out.v)
            sy.wait_ge(s.d_atw.h, s.d_atw.v)

        @blk.tensor
        def _(t):
            t.wait_ge(s.dve_p5.h, 32)
            for tt in range(8):
                if tt >= 2:
                    t.wait_ge(s.dve_p6.h, tt - 1)
                last = None
                for dc in range(DC):
                    last = t.transpose(tp_ps[tt % 2][:, 128 * dc:128 * (dc + 1)],
                                       sg_sb[:, T * dc + 128 * tt:T * dc + 128 * (tt + 1)],
                                       ident[:, :])
                s.pe_p6.inc(last)

        @blk.vector
        def _(v):
            v = Chained(v, s.ch_dve, ch_state, 'v')
            v.wait_ge(s.dve_p5.h, 32)
            for tt in range(8):
                v.wait_ge(s.pe_p6.h, tt + 1)
                v.wait_ge(d_xf[tt % 2].h, wv_x[("xf", tt)])
                if tt >= 2:
                    v.wait_ge(d_yw[tt % 2].h, wv_x[("yw", tt - 2)])
                v.tensor_tensor(out_sb[tt % 2][:, :], tp_ps[tt % 2][:, :],
                                xt2[tt % 2][:, :], ALU.add, _sem=s.dve_p6)

        p6.close()

    return nc


def _finish(nc, blk, s, y):
    with nc.sbuf_tensor("dummy_fin", [128, 512], F32) as dummy:
        @blk.vector
        def _(v):
            s.dve_p6.inc(v.memset(dummy[:, :], 0.0))

        @blk.sync
        def _(sy):
            sy.wait_ge(s.dve_p6.h, s.dve_p6.v)
            for tt in range(8):
                for cc in range(4):
                    s.d_out.inc(sy.dma_start(
                        out=y[128 * tt:128 * (tt + 1), 512 * cc:512 * (cc + 1)],
                        in_=dummy[:, :]), 16)
            sy.wait_ge(s.d_out.h, s.d_out.v)



_PERM = np.concatenate([np.arange(0, 128, 2), np.arange(1, 128, 2)])


def _host_prep(inputs):
    x = np.asarray(inputs["L0_attn_x"], dtype=np.float32)
    Wq = np.asarray(inputs["L0_attn_Wq"], dtype=np.float32)
    Wk = np.asarray(inputs["L0_attn_Wk"], dtype=np.float32)
    Wv = np.asarray(inputs["L0_attn_Wv"], dtype=np.float32)
    Wo = np.asarray(inputs["L0_attn_Wo"], dtype=np.float32)
    Aq = np.asarray(inputs["L0_attn_Aq"], dtype=np.float32)
    Ak = np.asarray(inputs["L0_attn_Ak"], dtype=np.float32)
    Av = np.asarray(inputs["L0_attn_Av"], dtype=np.float32)
    Ao = np.asarray(inputs["L0_attn_Ao"], dtype=np.float32)
    Bq = np.asarray(inputs["L0_attn_Bq"], dtype=np.float32)
    Bk = np.asarray(inputs["L0_attn_Bk"], dtype=np.float32)
    Bv = np.asarray(inputs["L0_attn_Bv"], dtype=np.float32)
    Bo = np.asarray(inputs["L0_attn_Bo"], dtype=np.float32)
    gate_w = np.asarray(inputs["L0_ffn_gate_w"], dtype=np.float32)
    up_w = np.asarray(inputs["L0_ffn_up_w"], dtype=np.float32)
    down_w = np.asarray(inputs["L0_ffn_down_w"], dtype=np.float32)
    gate_s = np.asarray(inputs["L0_ffn_gate_s"], dtype=np.float32)
    up_s = np.asarray(inputs["L0_ffn_up_s"], dtype=np.float32)
    down_s = np.asarray(inputs["L0_ffn_down_s"], dtype=np.float32)
    starts = np.asarray(inputs["L0_attn_lora_start_indices"], dtype=np.int32)

    wosw = np.ascontiguousarray(
        Wo.T.reshape(16, 128, 16, 128).transpose(2, 1, 0, 3).reshape(16, 128, D))
    gsw = np.ascontiguousarray(
        gate_w.reshape(IT, 128, 16, 128).transpose(0, 3, 2, 1).reshape(IT, 128, D))
    usw = np.ascontiguousarray(
        up_w.reshape(IT, 128, 16, 128).transpose(0, 3, 2, 1).reshape(IT, 128, D))
    dsw = np.ascontiguousarray(
        down_w.reshape(16, 128, IT, 128).transpose(0, 3, 2, 1).reshape(16, 128, INTER))
    gs_t = np.ascontiguousarray(gate_s.reshape(IT, 128).T)
    us_t = np.ascontiguousarray(up_s.reshape(IT, 128).T)
    ds_t = np.ascontiguousarray(down_s.reshape(16, 128).T)
    aqT = np.ascontiguousarray(Aq.transpose(2, 0, 1).reshape(D, 64))
    akT = np.ascontiguousarray(Ak.transpose(2, 0, 1).reshape(D, 64))
    avT = np.ascontiguousarray(Av.transpose(2, 0, 1).reshape(D, 64))
    aoT = np.ascontiguousarray(Ao.transpose(2, 0, 1).reshape(D, 64))
    boT = np.ascontiguousarray(Bo.transpose(0, 2, 1).reshape(64, D))

    pos = np.arange(S, dtype=np.float64)
    inv = 10000.0 ** (-np.arange(0, 128, 2, dtype=np.float64) / 128.0)
    ang = inv[:, None] * pos[None, :]
    cosT = np.cos(ang).astype(np.float32)
    sinT = np.sin(ang).astype(np.float32)

    jj = np.arange(512)[None, :]
    ii = np.arange(128)[:, None]
    masks = np.stack([(ii <= jj - 128 * m).astype(np.float32) for m in range(4)])
    masks = masks.astype(ml_dtypes.bfloat16)

    ident = np.eye(128, dtype=np.float32)
    s4 = np.tile(np.eye(16, dtype=np.float32), (4, 1))
    rep4 = np.repeat(np.eye(4, dtype=np.float32), 16, axis=1)
    c4row = np.arange(4, dtype=np.float32).reshape(1, 4)

    in_maps = []
    for c in range(8):
        g, r = c // 4, c % 4
        heads = [4 * r + hl for hl in range(NHL)]
        rows_q = np.concatenate([128 * H + _PERM for H in heads])
        rows_v = np.concatenate([128 * H + np.arange(128) for H in heads])
        im = {
            "x_sl": np.ascontiguousarray(x[g, T * r:T * (r + 1), :]),
            "wqT": np.ascontiguousarray(Wq[rows_q, :].T),
            "wkT": np.ascontiguousarray(Wk[rows_q, :].T),
            "wvT": np.ascontiguousarray(Wv[rows_v, :].T),
            "wosw": wosw, "gsw": gsw, "usw": usw, "dsw": dsw,
            "gs_t": gs_t, "us_t": us_t, "ds_t": ds_t,
            "aqT": aqT, "akT": akT, "avT": avT, "aoT": aoT,
            "bqT": np.ascontiguousarray(Bq[:, rows_q, :].transpose(0, 2, 1).reshape(64, 512)),
            "bkT": np.ascontiguousarray(Bk[:, rows_q, :].transpose(0, 2, 1).reshape(64, 512)),
            "bvT": np.ascontiguousarray(Bv[:, rows_v, :].transpose(0, 2, 1).reshape(64, 512)),
            "boT": boT,
            "starts": starts,
            "bsel": np.array([[g + 1.0]], dtype=np.float32),
            "toff": np.array([[T * r]], dtype=np.int32),
            "cosT": cosT, "sinT": sinT, "masks": masks, "ident": ident,
            "s4": s4, "rep4": rep4, "c4row": c4row,
        }
        in_maps.append(im)
    return in_maps


def kernel(**inputs):
    in_maps = _host_prep(inputs)
    nc = build_nc()
    res = run_bass_kernel_spmd(nc, in_maps, core_ids=list(range(8)))
    out = np.empty((2, S, D), dtype=np.float32)
    for c in range(8):
        g, r = c // 4, c % 4
        out[g, T * r:T * (r + 1), :] = np.asarray(res.results[c]["y"], dtype=np.float32)
    return out


# version 17
# speedup vs baseline: 1.0103x; 1.0103x over previous
import math
from contextlib import ExitStack

import numpy as np
import ml_dtypes

import concourse.bass as bass
import concourse.mybir as mybir
from concourse.bass_utils import run_bass_kernel_spmd

F32 = mybir.dt.float32
BF16 = mybir.dt.bfloat16
I32 = mybir.dt.int32
AF = mybir.ActivationFunctionType
ALU = mybir.AluOpType

D = 2048
S = 4096
T = 1024
DC = D // 128
NHL = 4
INTER = 5632
IT = INTER // 128
HT = DC
RK = 16
SCALE = 1.0 / math.sqrt(128.0)


class Sem:
    def __init__(self, nc, stack, name):
        self.h = stack.enter_context(nc.semaphore(name))
        self.v = 0

    def inc(self, instr, n=1):
        instr.then_inc(self.h, n)
        self.v += n
        return instr


class Chained:

    def __init__(self, eng, chsem, state, key):
        self._e = eng
        self._ch = chsem
        self._st = state
        self._k = key

    def wait_ge(self, h, v):
        return self._e.wait_ge(h, v)

    def __getattr__(self, name):
        f = getattr(self._e, name)

        def g(*a, _sem=None, **k):
            prev = self._st.get(self._k)
            if prev is not None:
                self._e.wait_ge(prev[0], prev[1])
            ins = f(*a, **k)
            sem = _sem if _sem is not None else self._ch
            sem.inc(ins)
            self._st[self._k] = (sem.h, sem.v)
            return ins

        return g


def build_nc(upto=None, dbg=None):
    nc = bass.Bass("TRN2", target_bir_lowering=False)

    def din(name, shape, dt=F32):
        return nc.dram_tensor(name, shape, dt, kind="ExternalInput").ap()

    x_sl = din("x_sl", [T, D])
    wqT = din("wqT", [D, 512])
    wkT = din("wkT", [D, 512])
    wvT = din("wvT", [D, 512])
    wosw = din("wosw", [16, 128, D])
    gsw = din("gsw", [IT, 128, D])
    usw = din("usw", [IT, 128, D])
    dsw = din("dsw", [HT, 128, INTER])
    gs_t = din("gs_t", [128, IT])
    us_t = din("us_t", [128, IT])
    ds_t = din("ds_t", [128, HT])
    aqT = din("aqT", [D, 64])
    akT = din("akT", [D, 64])
    avT = din("avT", [D, 64])
    aoT = din("aoT", [D, 64])
    bqT = din("bqT", [64, 512])
    bkT = din("bkT", [64, 512])
    bvT = din("bvT", [64, 512])
    boT = din("boT", [64, D])
    starts = din("starts", [4], I32)
    bsel = din("bsel", [1, 1])
    toff = din("toff", [1, 1], I32)
    cosT = din("cosT", [64, S])
    sinT = din("sinT", [64, S])
    masksI = din("masks", [4, 128, 512], BF16)
    identI = din("ident", [128, 128])
    s4I = din("s4", [64, 16])
    rep4I = din("rep4", [4, 64])
    c4I = din("c4row", [1, 4])

    y = nc.dram_tensor("y", [T, D], F32, kind="ExternalOutput").ap()

    cc_xn_in = nc.dram_tensor("cc_xn_in", [T, D], BF16)
    cc_xn_out = nc.dram_tensor("cc_xn_out", [S, D], BF16)
    cc_ai = [nc.dram_tensor(f"cc_ai{h}", [128, S], BF16) for h in range(NHL)]
    cc_ao = [nc.dram_tensor(f"cc_ao{h}", [512, S], BF16) for h in range(NHL)]
    hdn_d = nc.dram_tensor("hdn_d", [IT, 128, T], BF16)

    dbg_out = None
    if dbg is not None:
        dbg_shapes = {
            "xn": ([S, D], BF16), "xnl": ([T, D], BF16),
            "qk": ([128, 2 * NHL * S], BF16), "v": ([128, NHL * S], BF16),
            "low": ([80, S], BF16), "att": ([NHL * 512, S], BF16),
            "hdn": ([IT, 128, T], BF16), "gt": ([128, HT * T], F32),
            "sg": ([128, HT * T], F32),
        }
        shp, dt = dbg_shapes[dbg]
        dbg_out = nc.dram_tensor("dbg", shp, dt, kind="ExternalOutput").ap()

    RG = [[0, 1, 2, 3], [4, 5, 6, 7]]

    with ExitStack() as st:
        sems = {}
        for n in ("d_const d_cs d_cs1 d_x0 d_x1 d_x2 d_x3 d_xnw0 d_xnw1 d_xnsl d_xnt0 d_xnt1 "
                  "d_hdnw0 d_hdnw1 d_hdnr d_atw d_atr "
                  "d_wg0 d_wg1 d_wg2 d_wg3 d_wqkv d_wv d_wd0 d_wd1 d_wo0 d_wo1 d_xf0 d_xf1 "
                  "d_out cc0 cca0 cca1 cca2 cca3 "
                  "act_sq act_ln act_rs act_b act_p1 act_exp act_p4 "
                  "dve_st dve_vs dve_ln dve_b1 dve_b2 dve_b3 dve_b4 dve_boot "
                  "dve_p1 dve_p2 dve_low dve_p2b dve_msk dve_rl dve_att "
                  "dve_lo dve_p4 dve_p5 dve_p6 "
                  "pe_b pe_p1 pe_p2 pe_p2b pe_sc pe_av pe_avc pe_rb pe_p4 pe_tp dve_tp "
                  "pe_lo pe_p5 pe_p6 pool_ms ch_dve ch_act").split():
            sems[n] = Sem(nc, st, n)
        s = type("SS", (), sems)
        cca = [s.cca0, s.cca1, s.cca2, s.cca3]
        d_x = [s.d_x0, s.d_x1, s.d_x2, s.d_x3]
        d_xnw = [s.d_xnw0, s.d_xnw1]
        d_xnt = [s.d_xnt0, s.d_xnt1]
        d_hdnw = [s.d_hdnw0, s.d_hdnw1]
        d_wg = [s.d_wg0, s.d_wg1, s.d_wg2, s.d_wg3]
        d_wd = [s.d_wd0, s.d_wd1]
        d_wo = [s.d_wo0, s.d_wo1]
        d_xf = [s.d_xf0, s.d_xf1]
        wv_x = {}
        ch_state = {}

        def sb(name, shape, dt, **kw):
            return st.enter_context(nc.sbuf_tensor(name, shape, dt, **kw))

        big = st.enter_context(nc.sbuf_tensor("big_arena", [128, 94208], BF16,
                                              side="right"))

        def mt(name, shape, dt, addr, **kw):
            off = (addr - 40960) // 2
            cols_bf = shape[1] * ({F32: 4, BF16: 2, I32: 4}[dt] // 2)
            ap = big[0:shape[0], off:off + cols_bf]
            if dt != BF16:
                ap = ap.bitcast(dt)
            return ap

        A_Q, A_K, A_V = 40960, 73728, 106496
        A_HDN = 139264
        A_XT = 139264
        A_AT4 = 172032
        A_BT = 188416
        A_SQ = 204800
        A_XN = 212992
        A_XNSL = 94208
        A_GUB = 40960
        A_SGM = 73728
        A_T1B = 81920
        A_HSB = 86016
        A_XNT = 139264
        A_WV = 172032
        A_CST = 188416
        A_RT = 192512
        A_ATS = 196608
        A_PT = 204800
        A_MSK = 208896
        A_RL = 212992
        A_RB = 215040
        A_LO = 61440
        A_WQ2 = 196608
        A_WK2 = 212992
        A_DBUF = 40960
        A_GT = 63488
        A_DSG = 129024
        A_WO = 40960
        A_SG = 129024
        A_ATT = 194560
        A_XT2 = 49152
        A_OSB = 65536

        ident = sb("identS", [128, 128], F32)
        identb = sb("identb", [128, 128], BF16)
        ones128b = sb("ones128b", [128, 1], BF16)
        ones1x128 = sb("ones1x128", [1, 128], F32)
        gs_sb = sb("gs_sb", [128, IT], F32)
        us_sb = sb("us_sb", [128, IT], F32)
        ds_sb = sb("ds_sb", [128, HT], F32)
        c4row = sb("c4rowS", [1, 4], F32)
        st_i = sb("st_i", [1, 4], I32)
        st_f = sb("st_f", [1, 4], F32)
        bsel_sb = sb("bsel_sb", [1, 1], F32)
        toff_sb = sb("toff_sb", [1, 1], I32)
        s4_sb = sb("s4_sb", [64, 16], F32)
        rep4_sb = sb("rep4_sb", [4, 64], F32)
        u1 = sb("u1", [1, 4], F32)
        cnt = sb("cnt", [1, 1], F32)
        idv = sb("idv", [1, 1], F32)
        nidv = sb("nidv", [1, 1], F32)
        dtile = sb("dtile", [1, 4], F32)
        adtile = sb("adtile", [1, 4], F32)
        w4row = sb("w4row", [1, 4], F32)
        w4col = sb("w4col", [4, 1], F32)
        wcol = sb("wcol", [64, 1], F32)
        wb_sb = sb("wb_sb", [128, 4], F32)
        s4w = sb("s4w", [64, 16], F32)
        A_all = sb("A_all", [128, DC * 80], BF16)
        Aol = sb("Aol", [128, DC * RK], BF16)
        beff_pack = sb("beff_pack", [128, 512], BF16)
        beffo = sb("beffo", [16, D], BF16)
        low_all = sb("low_all", [128, S], BF16)
        cb_eps = sb("cb_eps", [128, 1], F32)
        cb_1 = sb("cb_1", [128, 1], F32)
        cb_m1 = sb("cb_m1", [128, 1], F32)

        blk = st.enter_context(nc.Block())
        toff_reg = st.enter_context(nc.sync.register("toffr"))
        toff_val = [None]

        xt = [mt(f"xt{i}", [128, D], F32, A_XT + 8192 * i) for i in range(4)]
        sq = mt("sqd", [128, D], F32, A_SQ)
        xn_sb = [mt(f"xn{i}", [128, D], BF16, A_XN + 4096 * i) for i in range(4)]
        xnsl = mt("xnsl", [128, DC * T], BF16, A_XNSL)
        xnsl3 = xnsl[:, :].rearrange("p (c e) -> p c e", e=T)
        msumW = sb("msumW", [128, 8], F32)
        vsumW = sb("vsumW", [128, 8], F32)
        nmeanW = sb("nmeanW", [128, 8], F32)
        m2W = sb("m2W", [128, 8], F32)
        veW = sb("veW", [128, 8], F32)
        lnvW = sb("lnvW", [128, 8], F32)
        rstdW = sb("rstdW", [128, 8], F32)
        aT_sb = {p: mt(f"aT_{p}", [128, DC * 64], F32, A_AT4 + 4096 * i)
                 for i, p in enumerate("qkvo")}
        bt_sb = {"q": mt("bt_q", [64, 512], F32, A_BT),
                 "k": mt("bt_k", [64, 512], F32, A_BT + 2048),
                 "v": mt("bt_v", [64, 512], F32, A_BT + 4096),
                 "o": mt("bt_o", [64, D], F32, A_BT + 6144)}
        selp = [sb(f"selp{i}", [128, 256], F32) for i in range(2)]
        p0ps = ExitStack()
        boot_ps = p0ps.enter_context(nc.psum_tensor("boot_ps", [128, 512], F32))
        tp_ps = [p0ps.enter_context(nc.psum_tensor(f"tp{i}", [128, 512], F32))
                 for i in range(2)]
        beff_ps = p0ps.enter_context(nc.psum_tensor("beff_ps", [16, D], F32))

        @blk.sync
        def _(sy):
            for tt in range(4):
                d_x[tt % 4].inc(sy.dma_start(out=xt[tt % 4][:, :],
                                             in_=x_sl[128 * tt:128 * (tt + 1), :]), 16)
                wv_x[("x", tt)] = d_x[tt % 4].v
            for dst, src in ((ident, identI), (gs_sb, gs_t), (us_sb, us_t),
                             (ds_sb, ds_t), (c4row, c4I), (bsel_sb, bsel),
                             (toff_sb, toff), (s4_sb, s4I), (rep4_sb, rep4I)):
                s.d_const.inc(sy.dma_start(out=dst[:, :], in_=src), 16)
            s.d_const.inc(sy.dma_start(
                out=st_i[:, :], in_=starts.rearrange("(one f) -> one f", one=1)), 16)
            for p, src in (("q", aqT), ("k", akT), ("v", avT), ("o", aoT)):
                s.d_const.inc(sy.dma_start(
                    out=aT_sb[p][:, :].rearrange("p (c r) -> p c r", r=64),
                    in_=src.rearrange("(c p) r -> p c r", p=128)), 16)
            for p, src in (("q", bqT), ("k", bkT), ("v", bvT), ("o", boT)):
                s.d_const.inc(sy.dma_start(out=bt_sb[p][:, :], in_=src), 16)
            sy.wait_ge(s.d_const.h, s.d_const.v)
            sy.reg_load(toff_reg, toff_sb[0:1, 0:1])
            toff_val[0] = sy.snap(toff_reg)
            for tt in range(8):
                sy.wait_ge(s.dve_ln.h, tt + 1)
                d_xnw[tt % 2].inc(sy.dma_start(
                    out=cc_xn_in.ap()[128 * tt:128 * (tt + 1), :],
                    in_=xn_sb[tt % 4][:, :]), 16)
                wv_x[("xnw", tt)] = d_xnw[tt % 2].v
                if tt + 4 < 8:
                    d_x[tt % 4].inc(sy.dma_start(
                        out=xt[tt % 4][:, :],
                        in_=x_sl[128 * (tt + 4):128 * (tt + 5), :]), 16)
                    wv_x[("x", tt + 4)] = d_x[tt % 4].v

        CONST_N = sems["d_const"].v

        @blk.vector
        def _(v):
            v = Chained(v, s.ch_dve, ch_state, 'v')

            def ln_dwave(w):
                c = slice(2 * w, 2 * w + 2)
                v.wait_ge(s.act_sq.h, w + 1)
                v.scalar_tensor_tensor(veW[:, c], vsumW[:, c], 1.0 / D, m2W[:, c],
                                       ALU.mult, ALU.subtract, _sem=s.dve_st)
                v.wait_ge(s.act_rs.h, w + 1)
                for tt in range(2 * w, 2 * w + 2):
                    if tt >= 4:
                        v.wait_ge(d_xnw[tt % 2].h, wv_x[("xnw", tt - 2)])
                        v.wait_ge(s.pe_tp.h, 4 * (tt - 4) + 4)
                    v.tensor_scalar(xn_sb[tt % 4][:, :], xt[tt % 4][:, :],
                                    nmeanW[:, tt:tt + 1], rstdW[:, tt:tt + 1],
                                    ALU.add, ALU.mult, _sem=s.dve_ln)
                for g in range(8 * w, 8 * w + 8):
                    tt, q = g // 4, g % 4
                    v.wait_ge(s.pe_tp.h, g + 1)
                    v.tensor_copy(xnsl3[:, 4 * q:4 * q + 4, 128 * tt:128 * (tt + 1)],
                                  tp_ps[g % 2][:, :].rearrange("p (c e) -> p c e", e=128),
                                  _sem=s.dve_tp)
            ln_dwave(0)
            ln_dwave(1)
            v.wait_ge(s.d_const.h, CONST_N)
            v.tensor_copy(st_f[:, :], st_i[:, :])
            v.tensor_scalar(u1[:, :], st_f[:, :], -1.0, bsel_sb[0:1, 0:1], ALU.mult, ALU.add)
            v.tensor_scalar(u1[:, :], u1[:, :], 0.0, 1.0, ALU.max, ALU.min)
            v.reduce_sum(cnt[:, :], u1[:, :], axis=mybir.AxisListType.X, _sem=s.dve_b1)
            v.wait_ge(s.act_b.h, 2)
            v.tensor_scalar_add(dtile[:, :], c4row[:, :], nidv[0:1, 0:1], _sem=s.dve_b2)
            v.wait_ge(s.pe_b.h, 1)
            v.tensor_copy(w4col[:, :], boot_ps[0:4, 0:1], _sem=s.dve_b3)
            v.wait_ge(s.pe_b.h, 2)
            v.tensor_copy(wcol[:, :], boot_ps[0:64, 4:5])
            v.tensor_copy(wb_sb[:, :], boot_ps[:, 8:12])
            v.tensor_scalar_mul(s4w[:, :], s4_sb[:, :], wcol[:, 0:1], _sem=s.dve_b4)
            v.wait_ge(s.pe_b.h, 3)
            v.tensor_copy(beff_pack[0:16, :], beff_ps[0:16, 0:512])
            v.tensor_copy(beff_pack[32:48, :], beff_ps[0:16, 512:1024])
            v.tensor_copy(beff_pack[64:80, :], beff_ps[0:16, 1024:1536], _sem=s.dve_b4)
            v.wait_ge(s.pe_b.h, 4)
            v.tensor_copy(beffo[:, :], beff_ps[0:16, :])
            ln_dwave(2)
            ln_dwave(3)
            v.wait_ge(s.pool_ms.h, 1)
            offs = {"q": 0, "k": 32, "v": 64}
            t03 = selp[0][:, :].rearrange("p (c e) -> p c e", e=16)
            t13 = selp[1][:, :].rearrange("p (c e) -> p c e", e=16)
            A3 = A_all[:, :].rearrange("p (c e) -> p c e", e=80)
            Ao3 = Aol[:, :].rearrange("p (c e) -> p c e", e=RK)
            for p in "qkvo":
                a3 = aT_sb[p][:, :].rearrange("p (c e) -> p c e", e=64)
                v.tensor_scalar_mul(t03, a3[:, :, 0:16], wb_sb[:, 0:1])
                v.scalar_tensor_tensor(t13, a3[:, :, 16:32], wb_sb[:, 1:2], t03,
                                       ALU.mult, ALU.add)
                v.scalar_tensor_tensor(t03, a3[:, :, 32:48], wb_sb[:, 2:3], t13,
                                       ALU.mult, ALU.add)
                dst = Ao3[:, :, :] if p == "o" else A3[:, :, offs[p]:offs[p] + 16]
                v.scalar_tensor_tensor(dst, a3[:, :, 48:64], wb_sb[:, 3:4], t03,
                                       ALU.mult, ALU.add,
                                       _sem=(s.dve_boot if p == "o" else None))

        @blk.scalar
        def _(sc):
            sc = Chained(sc, s.ch_act, ch_state, 'a')
            sc.wait_ge(s.pool_ms.h, 1)
            def ln_wave(w):
                for t2 in range(2):
                    tt = 2 * w + t2
                    sc.wait_ge(d_x[tt % 4].h, wv_x[("x", tt)])
                    sc.activation(sq[:, :], xt[tt % 4][:, :], AF.Copy, bias=0.0,
                                  scale=1.0, accum_out=msumW[:, tt:tt + 1])
                    sc.activation(sq[:, :], xt[tt % 4][:, :], AF.Square, bias=0.0,
                                  scale=1.0, accum_out=vsumW[:, tt:tt + 1])
                c = slice(2 * w, 2 * w + 2)
                sc.activation(nmeanW[:, c], msumW[:, c], AF.Copy, bias=0.0,
                              scale=-1.0 / D)
                sc.activation(m2W[:, c], nmeanW[:, c], AF.Square, bias=0.0,
                              scale=1.0, _sem=s.act_sq)
                sc.wait_ge(s.dve_st.h, w + 1)
                sc.activation(lnvW[:, c], veW[:, c], AF.Ln, bias=cb_eps[:, 0:1],
                              scale=1.0)
                sc.activation(rstdW[:, c], lnvW[:, c], AF.Exp, bias=0.0,
                              scale=-0.5, _sem=s.act_rs)
            ln_wave(0)
            ln_wave(1)
            sc.wait_ge(s.d_const.h, CONST_N)
            sc.wait_ge(s.dve_b1.h, 1)
            sc.activation(idv[:, :], cnt[:, :], AF.Relu, bias=cb_m1[0:1, 0:1], scale=1.0, _sem=s.act_b)
            sc.activation(nidv[:, :], idv[:, :], AF.Copy, bias=0.0, scale=-1.0, _sem=s.act_b)
            sc.wait_ge(s.dve_b2.h, 1)
            sc.activation(adtile[:, :], dtile[:, :], AF.Abs, bias=0.0, scale=1.0, _sem=s.act_b)
            sc.activation(w4row[:, :], adtile[:, :], AF.Relu, bias=cb_1[0:1, 0:1], scale=-1.0, _sem=s.act_b)
            ln_wave(2)
            ln_wave(3)

        def tp_tile(t, tt):
            t.wait_ge(s.dve_ln.h, tt + 1)
            for q in range(4):
                g = 4 * tt + q
                if g == 0:
                    t.wait_ge(s.act_ln.h, 16)
                if g >= 2:
                    t.wait_ge(s.dve_tp.h, g - 1)
                last = None
                for j in range(4):
                    dc = 4 * q + j
                    last = t.matmul(tp_ps[g % 2][:, 128 * j:128 * (j + 1)],
                                    xn_sb[tt % 4][:, 128 * dc:128 * (dc + 1)],
                                    identb[:, :], start=True, stop=True,
                                    skip_group_check=True)
                s.pe_tp.inc(last)

        @blk.tensor
        def _(t):
            for tt in range(4):
                tp_tile(t, tt)
            t.wait_ge(s.act_b.h, 4)
            s.pe_b.inc(t.matmul(boot_ps[0:4, 0:1], w4row[:, :], ones1x128[0:1, 0:1],
                                start=True, stop=True, skip_group_check=True))
            t.wait_ge(s.dve_b3.h, 1)
            t.matmul(boot_ps[0:64, 4:5], rep4_sb[:, :], w4col[:, :],
                     start=True, stop=True, skip_group_check=True)
            s.pe_b.inc(t.matmul(boot_ps[:, 8:12], ones1x128[:, :], w4row[:, :],
                                start=True, stop=True, skip_group_check=True))
            t.wait_ge(s.dve_b4.h, 1)
            t.matmul(beff_ps[0:16, 0:512], s4w[:, :], bt_sb["q"][:, :],
                     start=True, stop=True, skip_group_check=True)
            t.matmul(beff_ps[0:16, 512:1024], s4w[:, :], bt_sb["k"][:, :],
                     start=True, stop=True, skip_group_check=True)
            s.pe_b.inc(t.matmul(beff_ps[0:16, 1024:1536], s4w[:, :], bt_sb["v"][:, :],
                                start=True, stop=True, skip_group_check=True))
            t.wait_ge(s.dve_b4.h, 2)
            last = None
            for j in range(4):
                last = t.matmul(beff_ps[0:16, 512 * j:512 * (j + 1)], s4w[:, :],
                                bt_sb["o"][:, 512 * j:512 * (j + 1)],
                                start=True, stop=True, skip_group_check=True)
            s.pe_b.inc(last)
            for tt in range(4, 8):
                tp_tile(t, tt)

        @blk.gpsimd
        def _(g):
            s.act_ln.inc(g.dma_start(out=identb[:, :], in_=identI), 16)
            g.memset(ones128b[:, :], 1.0)
            g.memset(ones1x128[:, :], 1.0)
            g.memset(cb_eps[:, :], 1e-5)
            g.memset(cb_1[:, :], 1.0)
            g.memset(cb_m1[:, :], -1.0)
            g.memset(A_all[:, :], 0.0)
            s.pool_ms.inc(g.memset(Aol[:, :], 0.0))

        if dbg == "xnl":
            @blk.sync
            def _(sy):
                sy.wait_ge(d_xnw[0].h, 64)
                sy.wait_ge(d_xnw[1].h, 64)
                s.d_out.inc(sy.dma_start(out=dbg_out, in_=cc_xn_in.ap()[:, :]), 16)

        if upto == "p0":
            p0ps.close()
            _finish(nc, blk, s, y)
            return nc

        p0ps.close()
        xnt = [mt(f"xnt{i}", [128, DC * 512], BF16, A_XNT + 16384 * i) for i in range(2)]
        cst = [mt(f"cst{i}", [128, 512], F32, A_CST + 2048 * i) for i in range(2)]
        d_cs2 = [sems["d_cs"], sems["d_cs1"]]
        wv_sb = mt("wv_sb", [128, DC * 512], BF16, A_WV)
        wq_sb = mt("wq_sb", [128, DC * 512], BF16, A_WQ2)
        wk_sb = mt("wk_sb", [128, DC * 512], BF16, A_WK2)
        gbuf = [mt(f"gbuf{i}", [128, D], BF16, A_GUB + 4096 * i) for i in range(4)]
        pk_g = {it: mt(f"pkg{it}", [128, D], BF16, 172032 + 8192 * (it - 4))
                for it in range(4, 11)}
        pk_u = {it: mt(f"pku{it}", [128, D], BF16, 172032 + 8192 * (it - 4) + 4096)
                for it in range(4, 11)}
        pk_sem = {4: s.d_wg0, 5: s.d_wg1, 6: s.d_wg2, 7: s.d_wg3,
                  8: s.d_xf0, 9: s.d_xf1, 10: s.d_out}
        ubuf = [mt(f"ubuf{i}", [128, D], BF16, A_GUB + 16384 + 4096 * i) for i in range(4)]
        sgm = [mt(f"sgm{i}", [128, T], F32, A_SGM + 4096 * i) for i in range(2)]
        t1b = mt("t1b", [128, T], F32, A_T1B)
        hdn_sb = [mt(f"hdnsb{i}", [128, T], BF16, A_HSB + 2048 * i) for i in range(4)]
        p1 = ExitStack()
        gp_ps = [p1.enter_context(nc.psum_tensor(f"gp{i}", [128, T], F32)) for i in range(2)]
        up_ps = [p1.enter_context(nc.psum_tensor(f"up{i}", [128, T], F32)) for i in range(2)]

        @blk.gpsimd
        def _(g):
            for it in range(4):
                d_wg[it % 4].inc(g.dma_start(out=gbuf[it % 4][:, :], in_=gsw[it, :, :]), 16)
                d_wg[it % 4].inc(g.dma_start(out=ubuf[it % 4][:, :], in_=usw[it, :, :]), 16)
                wv_x[("wg", it)] = d_wg[it % 4].v
            g.wait_ge(d_xnw[0].h, 64)
            g.wait_ge(d_xnw[1].h, 64)
            s.cc0.inc(g.collective_compute(
                "AllGather", ALU.bypass, replica_groups=RG,
                ins=[cc_xn_in.ap()], outs=[cc_xn_out.ap()]))
            g.wait_ge(s.dve_boot.h, 1)
            for j in range(4):
                g.wait_ge(d_wg[j].h, wv_x[("wg", j)])
            for it in range(4, 11):
                sem = pk_sem[it]
                sem.inc(g.dma_start(out=pk_g[it][:, :], in_=gsw[it, :, :]), 16)
                sem.inc(g.dma_start(out=pk_u[it][:, :], in_=usw[it, :, :]), 16)
                wv_x[("park", it)] = sem.v
            for it in range(11, IT):
                g.wait_ge(s.pe_p1.h, it - 3)
                d_wg[it % 4].inc(g.dma_start(out=gbuf[it % 4][:, :], in_=gsw[it, :, :]), 16)
                d_wg[it % 4].inc(g.dma_start(out=ubuf[it % 4][:, :], in_=usw[it, :, :]), 16)
                wv_x[("wg", it)] = d_wg[it % 4].v
                if 16 <= it < 40:
                    qi = it - 16
                    wsb, wsrc, sem = ((wq_sb, wqT, s.d_wqkv), (wk_sb, wkT, s.dve_vs),
                                      (wv_sb, wvT, s.d_wv))[qi // 8]
                    q = qi % 8
                    sem.inc(g.dma_start(
                        out=wsb[:, 1024 * q:1024 * (q + 1)].rearrange(
                            "p (c f) -> p c f", f=512),
                        in_=wsrc[256 * q:256 * (q + 1), :].rearrange(
                            "(c p) f -> p c f", p=128)), 16)

        @blk.tensor
        def _(t):
            t.wait_ge(s.dve_boot.h, 1)
            t.wait_ge(s.dve_tp.h, 32)
            for it in range(IT):
                if 4 <= it <= 10:
                    t.wait_ge(pk_sem[it].h, wv_x[("park", it)])
                else:
                    t.wait_ge(d_wg[it % 4].h, wv_x[("wg", it)])
                if it >= 2:
                    t.wait_ge(s.act_p1.h, it - 1)
                    t.wait_ge(s.dve_p1.h, it - 1)
                b = it % 2
                last = None
                gt_, ut_ = ((pk_g[it], pk_u[it]) if 4 <= it <= 10
                            else (gbuf[it % 4], ubuf[it % 4]))
                for dc in range(DC):
                    gl = gt_[:, 128 * dc:128 * (dc + 1)]
                    ul = ut_[:, 128 * dc:128 * (dc + 1)]
                    for th in range(2):
                        rh = xnsl[:, T * dc + 512 * th:T * dc + 512 * (th + 1)]
                        t.matmul(gp_ps[b][:, 512 * th:512 * (th + 1)], gl, rh,
                                 start=(dc == 0), stop=(dc == DC - 1), skip_group_check=True)
                        last = t.matmul(up_ps[b][:, 512 * th:512 * (th + 1)], ul, rh,
                                        start=(dc == 0), stop=(dc == DC - 1),
                                        skip_group_check=True)
                s.pe_p1.inc(last)

        @blk.scalar
        def _(sc):
            sc = Chained(sc, s.ch_act, ch_state, 'a')
            for it in range(IT):
                b = it % 2
                sc.wait_ge(s.pe_p1.h, it + 1)
                if it >= 2:
                    sc.wait_ge(s.dve_p1.h, it - 1)
                sc.activation(sgm[b][:, :], gp_ps[b][:, :], AF.Sigmoid,
                              bias=0.0, scale=gs_sb[:, it:it + 1], _sem=s.act_p1)

        @blk.sync
        def _(sy):
            for it in range(IT):
                sy.wait_ge(s.dve_p1.h, it + 1)
                d_hdnw[it % 2].inc(sy.dma_start(out=hdn_d.ap()[it, :, :],
                                                in_=hdn_sb[it % 4][:, :]), 16)
                wv_x[("hdnw", it)] = d_hdnw[it % 2].v
                if 24 <= it < 32:
                    k = it - 24
                    tc, q4 = k // 4, k % 4
                    if k == 0:
                        sy.wait_ge(s.cc0.h, 1)
                    if q4 == 0:
                        d_cs2[tc].inc(sy.dma_start(
                            out=cst[tc][0:64, :],
                            in_=cosT[:, 512 * tc:512 * (tc + 1)]), 16)
                        d_cs2[tc].inc(sy.dma_start(
                            out=cst[tc][64:128, :],
                            in_=sinT[:, 512 * tc:512 * (tc + 1)]), 16)
                        wv_x[("cs", tc)] = d_cs2[tc].v
                    for dc in range(4 * q4, 4 * q4 + 4):
                        d_xnt[tc].inc(sy.dma_start(
                            out=xnt[tc][:, 512 * dc:512 * (dc + 1)],
                            in_=cc_xn_out.ap()[512 * tc:512 * (tc + 1),
                                               128 * dc:128 * (dc + 1)],
                            transpose=True), 16)
                    if q4 == 3:
                        wv_x[("xnt", tc)] = d_xnt[tc].v

        @blk.vector
        def _(v):
            v = Chained(v, s.ch_dve, ch_state, 'v')
            for it in range(IT):
                b = it % 2
                v.wait_ge(s.pe_p1.h, it + 1)
                v.wait_ge(s.act_p1.h, it + 1)
                if it >= 4:
                    v.wait_ge(d_hdnw[it % 2].h, wv_x[("hdnw", it - 2)])
                v.scalar_tensor_tensor(t1b[:, :], gp_ps[b][:, :], gs_sb[:, it:it + 1],
                                       sgm[b][:, :], ALU.mult, ALU.mult)
                v.scalar_tensor_tensor(
                    hdn_sb[it % 4][:, :], up_ps[b][:, :], us_sb[:, it:it + 1],
                    t1b[:, :], ALU.mult, ALU.mult, _sem=s.dve_p1)

        if dbg == "hdn":
            @blk.sync
            def _(sy):
                sy.wait_ge(d_hdnw[0].h, 16 * (IT // 2))
                sy.wait_ge(d_hdnw[1].h, 16 * (IT // 2))
                s.d_out.inc(sy.dma_start(out=dbg_out, in_=hdn_d.ap()[:, :, :]), 16)
        if dbg == "xn":
            @blk.sync
            def _(sy):
                sy.wait_ge(s.cc0.h, 1)
                s.d_out.inc(sy.dma_start(out=dbg_out, in_=cc_xn_out.ap()[:, :]), 16)

        p1.close()

        if upto == "p1":
            _finish(nc, blk, s, y)
            return nc

        rta = mt("rta", [64, 512], F32, A_RT)
        rtb = mt("rtb", [64, 512], F32, A_RT + 2048)
        Q_sb = mt("Q_sb", [128, NHL * S], BF16, A_Q)
        K_sb = mt("K_sb", [128, NHL * S], BF16, A_K)
        V_sb = mt("V_sb", [128, NHL * S], BF16, A_V)
        p2 = ExitStack()
        qk_ps = [p2.enter_context(nc.psum_tensor(f"qk{i}", [128, 512], F32)) for i in range(4)]
        NG = 13

        @blk.sync
        def _(sy):
            sy.wait_ge(s.pe_p1.h, IT)
            sy.wait_ge(s.dve_p1.h, IT)
            sy.wait_ge(d_hdnw[0].h, 16 * (IT // 2))
            sy.wait_ge(d_hdnw[1].h, 16 * (IT // 2))
            sy.wait_ge(s.cc0.h, 1)
            for tc in range(2, 8):
                sy.wait_ge(s.pe_p2.h, NG * (tc - 1))
                sy.wait_ge(s.dve_p2.h, NG * (tc - 1))
                d_cs2[tc % 2].inc(sy.dma_start(
                    out=cst[tc % 2][0:64, :], in_=cosT[:, 512 * tc:512 * (tc + 1)]), 16)
                d_cs2[tc % 2].inc(sy.dma_start(
                    out=cst[tc % 2][64:128, :], in_=sinT[:, 512 * tc:512 * (tc + 1)]), 16)
                wv_x[("cs", tc)] = d_cs2[tc % 2].v
                for dc in range(DC):
                    d_xnt[tc % 2].inc(sy.dma_start(
                        out=xnt[tc % 2][:, 512 * dc:512 * (dc + 1)],
                        in_=cc_xn_out.ap()[512 * tc:512 * (tc + 1), 128 * dc:128 * (dc + 1)],
                        transpose=True), 16)
                wv_x[("xnt", tc)] = d_xnt[tc % 2].v

        @blk.tensor
        def _(t):
            fi = 0
            for tc in range(8):
                t.wait_ge(d_xnt[tc % 2].h, wv_x[("xnt", tc)])
                for f in range(NG):
                    ps = qk_ps[fi % 4]
                    if fi >= 4:
                        t.wait_ge(s.dve_p2.h, fi - 3)
                    if f == 0:
                        last = None
                        for dc in range(DC):
                            last = t.matmul(ps[0:80, :], A_all[:, 80 * dc:80 * (dc + 1)],
                                            xnt[tc % 2][:, 512 * dc:512 * (dc + 1)],
                                            start=(dc == 0), stop=(dc == DC - 1),
                                            skip_group_check=True)
                        s.pe_p2.inc(last)
                    elif f <= 8:
                        qk = 0 if f <= 4 else 1
                        h = (f - 1) % 4
                        if fi == 1:
                            t.wait_ge(s.d_wqkv.h, 128)
                        if fi == 5:
                            t.wait_ge(s.dve_vs.h, 128)
                        wsb = wq_sb if qk == 0 else wk_sb
                        for dc in range(DC):
                            t.matmul(ps[:, :],
                                     wsb[:, 512 * dc + 128 * h:512 * dc + 128 * (h + 1)],
                                     xnt[tc % 2][:, 512 * dc:512 * (dc + 1)],
                                     start=(dc == 0), stop=False, skip_group_check=True)
                        t.wait_ge(s.dve_p2.h, NG * tc + 1)
                        pb = 0 if qk == 0 else 32
                        s.pe_p2.inc(t.matmul(
                            ps[:, :], beff_pack[pb:pb + 16, 128 * h:128 * (h + 1)],
                            low_all[pb:pb + 16, 512 * tc:512 * (tc + 1)],
                            start=False, stop=True, skip_group_check=True))
                    else:
                        j = f - 9
                        if fi == 9:
                            t.wait_ge(s.d_wv.h, 128)
                        for dc in range(DC):
                            t.matmul(ps[:, :],
                                     xnt[tc % 2][:, 512 * dc + 128 * j:512 * dc + 128 * (j + 1)],
                                     wv_sb[:, 512 * dc:512 * (dc + 1)],
                                     start=(dc == 0), stop=False, skip_group_check=True)
                        t.wait_ge(s.dve_p2.h, NG * tc + 1)
                        s.pe_p2.inc(t.matmul(
                            ps[:, :], low_all[64:80, 512 * tc + 128 * j:512 * tc + 128 * (j + 1)],
                            beff_pack[64:80, :], start=False, stop=True, skip_group_check=True))
                    fi += 1

        @blk.vector
        def _(v):
            v = Chained(v, s.ch_dve, ch_state, 'v')
            fi = 0
            for tc in range(8):
                v.wait_ge(d_cs2[tc % 2].h, wv_x[("cs", tc)])
                for f in range(NG):
                    ps = qk_ps[fi % 4]
                    v.wait_ge(s.pe_p2.h, fi + 1)
                    if f == 0:
                        v.tensor_copy(low_all[0:80, 512 * tc:512 * (tc + 1)],
                                      ps[0:80, :], _sem=s.dve_p2)
                    elif f <= 8:
                        qk = 0 if f <= 4 else 1
                        h = (f - 1) % 4
                        mat = Q_sb if qk == 0 else K_sb
                        col = S * h + 512 * tc
                        cs = cst[tc % 2][0:64, :]
                        sn = cst[tc % 2][64:128, :]
                        v.tensor_tensor(rta[:, :], ps[0:64, :], cs, ALU.mult)
                        v.tensor_tensor(rtb[:, :], ps[64:128, :], sn, ALU.mult)
                        v.tensor_tensor(mat[0:64, col:col + 512], rta[:, :], rtb[:, :],
                                        ALU.subtract)
                        v.tensor_tensor(rta[:, :], ps[0:64, :], sn, ALU.mult)
                        v.tensor_tensor(rtb[:, :], ps[64:128, :], cs, ALU.mult)
                        v.tensor_tensor(mat[64:128, col:col + 512],
                                        rta[:, :], rtb[:, :], ALU.add, _sem=s.dve_p2)
                    else:
                        j = f - 9
                        col = 128 * (4 * tc + j)
                        for h in range(NHL):
                            v.tensor_copy(V_sb[:, S * h + col:S * h + col + 128],
                                          ps[:, 128 * h:128 * (h + 1)],
                                          _sem=(s.dve_p2 if h == NHL - 1 else None))
                    fi += 1

        P2_TOT = 8 * NG
        p2.close()
        hdn_all = mt("hdn_all", [128, IT * T], BF16, A_HDN)

        @blk.sync
        def _(sy):
            sy.wait_ge(s.pe_p2.h, P2_TOT)
            sy.wait_ge(s.dve_p2.h, P2_TOT)
            for it in list(range(0, 28)) + list(range(38, IT)):
                s.d_hdnr.inc(sy.dma_start(out=hdn_all[:, T * it:T * (it + 1)],
                                          in_=hdn_d.ap()[it, :, :]), 16)

        if dbg == "qk":
            @blk.sync
            def _(sy):
                sy.wait_ge(s.dve_p2.h, P2_TOT)
                s.d_out.inc(sy.dma_start(out=dbg_out[:, 0:NHL * S], in_=Q_sb[:, :]), 16)
                s.d_out.inc(sy.dma_start(out=dbg_out[:, NHL * S:2 * NHL * S], in_=K_sb[:, :]), 16)
        if dbg == "v":
            @blk.sync
            def _(sy):
                sy.wait_ge(s.dve_p2.h, P2_TOT)
                s.d_out.inc(sy.dma_start(out=dbg_out, in_=V_sb[:, :]), 16)
        if dbg == "low":
            @blk.sync
            def _(sy):
                sy.wait_ge(s.dve_p2.h, NG * 7 + 1)
                s.d_out.inc(sy.dma_start(out=dbg_out[0:80, :], in_=low_all[0:80, :]), 16)
        if upto == "p2":
            _finish(nc, blk, s, y)
            return nc

        p3 = ExitStack()
        pt_sb = [mt(f"pt{i}", [128, 512], BF16, A_PT + 1024 * i) for i in range(4)]
        masks = mt("masksS", [128, 4 * 512], BF16, A_MSK)
        at_sb = mt("at_sb", [128, S], BF16, A_ATS)
        rl_sb = mt("rl_sb", [1, 512], F32, A_RL)
        rb_sb = mt("rb_sb", [128, 512], F32, A_RB)
        sc_ps = [p3.enter_context(nc.psum_tensor(f"scps{i}", [128, 512], F32)) for i in range(4)]
        av_ps = [p3.enter_context(nc.psum_tensor(f"avps{i}", [128, 512], F32)) for i in range(2)]
        l_ps = p3.enter_context(nc.psum_tensor("l_ps", [1, 512], F32))
        rb_ps = p3.enter_context(nc.psum_tensor("rb_ps", [128, 512], F32))

        dbuf = [mt(f"dbuf{i}", [128, INTER], BF16, A_DBUF + 11264 * i) for i in range(2)]
        sc_cnt = {}
        avc_cnt = {}
        msk_cnt = {}
        CH = 0
        AVC = 0
        MSK = 0
        chpar = {}
        for h in range(NHL):
            for qb in range(8):
                nch = 4 * qb + 4
                for c in range(nch):
                    CH += 1
                    sc_cnt[(h, qb, c)] = CH
                    chpar[(h, qb, c)] = (CH - 1) % 4
                    AVC += 1
                    avc_cnt[(h, qb, c)] = AVC
                    if c >= 4 * qb:
                        MSK += 1
                    msk_cnt[(h, qb, c)] = MSK
        SC_TOT = CH

        @blk.sync
        def _(sy):
            sy.wait_ge(s.dve_p2.h, P2_TOT)
            s.d_const.inc(sy.dma_start(
                out=masks[:, :].rearrange("p (m f) -> p m f", f=512),
                in_=masksI.rearrange("m p f -> p m f")), 16)

        @blk.tensor
        def _(t):
            t.wait_ge(s.dve_p2.h, P2_TOT)
            QBc = 0
            rb_pending = [None]
            for h in range(NHL):
                for qb in range(8):
                    nch = 4 * qb + 4
                    qcol = S * h + 512 * qb

                    def av_chunk(c, QBc=QBc, h=h, qb=qb, nch=nch):
                        if c == 0:
                            t.wait_ge(s.dve_rl.h, QBc)
                            if QBc >= 2:
                                t.wait_ge(s.dve_att.h, QBc - 1)
                        t.wait_ge(s.act_exp.h, sc_cnt[(h, qb, c)])
                        if c >= 4 * qb:
                            t.wait_ge(s.dve_msk.h, msk_cnt[(h, qb, c)])
                        pt = pt_sb[chpar[(h, qb, c)]][:, :]
                        t.matmul(av_ps[QBc % 2][:, :],
                                 V_sb[:, S * h + 128 * c:S * h + 128 * (c + 1)],
                                 pt, start=(c == 0), stop=(c == nch - 1),
                                 skip_group_check=True)
                        last = t.matmul(l_ps[:, :], ones128b[:, :], pt,
                                        start=(c == 0), stop=(c == nch - 1),
                                        skip_group_check=True)
                        s.pe_avc.inc(last)

                    for c in range(nch):
                        gch = sc_cnt[(h, qb, c)]
                        if gch >= 5:
                            t.wait_ge(s.act_exp.h, gch - 4)
                        last = t.matmul(
                            sc_ps[chpar[(h, qb, c)]][:, :],
                            K_sb[:, S * h + 128 * c:S * h + 128 * (c + 1)],
                            Q_sb[:, qcol:qcol + 512],
                            start=True, stop=True, skip_group_check=True)
                        s.pe_sc.inc(last)
                        if c == 1 and rb_pending[0] is not None:
                            t.wait_ge(s.dve_rl.h, rb_pending[0] + 1)
                            if rb_pending[0] >= 1:
                                t.wait_ge(s.dve_att.h, rb_pending[0])
                            s.pe_rb.inc(t.matmul(rb_ps[:, :], ones1x128[:, :],
                                                 rl_sb[:, :], start=True, stop=True,
                                                 skip_group_check=True))
                            rb_pending[0] = None
                        if c >= 2:
                            av_chunk(c - 2)
                    av_chunk(nch - 2)
                    av_chunk(nch - 1)
                    rb_pending[0] = QBc
                    QBc += 1
            t.wait_ge(s.dve_rl.h, rb_pending[0] + 1)
            t.wait_ge(s.dve_att.h, rb_pending[0])
            s.pe_rb.inc(t.matmul(rb_ps[:, :], ones1x128[:, :], rl_sb[:, :],
                                 start=True, stop=True, skip_group_check=True))

        @blk.scalar
        def _(sc):
            sc = Chained(sc, s.ch_act, ch_state, 'a')
            for h in range(NHL):
                for qb in range(8):
                    nch = 4 * qb + 4
                    for c in range(nch):
                        gch = sc_cnt[(h, qb, c)]
                        sc.wait_ge(s.pe_sc.h, gch)
                        if gch >= 5:
                            sc.wait_ge(s.pe_avc.h, gch - 4)
                        b = chpar[(h, qb, c)]
                        sc.activation(pt_sb[b][:, :], sc_ps[b][:, :], AF.Exp,
                                      bias=0.0, scale=SCALE, _sem=s.act_exp)

        @blk.vector
        def _(v):
            v = Chained(v, s.ch_dve, ch_state, 'v')
            v.wait_ge(s.d_const.h, s.d_const.v)
            QBc = 0
            for h in range(NHL):
                for qb in range(8):
                    nch = 4 * qb + 4
                    for c in range(4 * qb, nch):
                        gch = sc_cnt[(h, qb, c)]
                        m = c - 4 * qb
                        v.wait_ge(s.act_exp.h, gch)
                        b = chpar[(h, qb, c)]
                        v.tensor_tensor(pt_sb[b][:, :], pt_sb[b][:, :],
                                        masks[:, 512 * m:512 * (m + 1)],
                                        ALU.mult, _sem=s.dve_msk)
                    v.wait_ge(s.pe_avc.h, avc_cnt[(h, qb, nch - 1)])
                    v.reciprocal(rl_sb[:, :], l_ps[:, :], _sem=s.dve_rl)
                    v.wait_ge(s.pe_rb.h, QBc + 1)
                    v.tensor_copy(rb_sb[:, :], rb_ps[:, :])
                    if h >= 1:
                        v.wait_ge(s.d_atw.h, 16 * (8 * (h - 1) + qb + 1))
                    v.tensor_tensor(
                        at_sb[:, 512 * qb:512 * (qb + 1)], av_ps[QBc % 2][:, :],
                        rb_sb[:, :], ALU.mult, _sem=s.dve_att)
                    QBc += 1

        @blk.sync
        def _(sy):
            for h in range(NHL):
                for qb in range(8):
                    sy.wait_ge(s.dve_att.h, 8 * h + qb + 1)
                    s.d_atw.inc(sy.dma_start(
                        out=cc_ai[h].ap()[:, 512 * qb:512 * (qb + 1)],
                        in_=at_sb[:, 512 * qb:512 * (qb + 1)]), 16)
                    sy.wait_ge(s.d_atw.h, s.d_atw.v)
            for it in range(28, 38):
                s.d_hdnr.inc(sy.dma_start(out=hdn_all[:, T * it:T * (it + 1)],
                                          in_=hdn_d.ap()[it, :, :]), 16)

        @blk.gpsimd
        def _(g):
            for h in range(NHL):
                g.wait_ge(s.d_atw.h, 16 * 8 * (h + 1))
                cca[h].inc(g.collective_compute(
                    "AllGather", ALU.bypass, replica_groups=RG,
                    ins=[cc_ai[h].ap()], outs=[cc_ao[h].ap()]))
                if h in (1, 2):
                    g.wait_ge(s.pe_sc.h, 144 * (h + 1))
                    d_wd[h - 1].inc(g.dma_start(out=dbuf[h - 1][:, :],
                                                in_=dsw[h - 1, :, :]), 16)
                    wv_x[("wd", h - 1)] = d_wd[h - 1].v

        p3.close()

        if dbg == "att":
            @blk.sync
            def _(sy):
                for h in range(NHL):
                    sy.wait_ge(cca[h].h, 1)
                    s.d_out.inc(sy.dma_start(out=dbg_out[512 * h:512 * (h + 1), :],
                                             in_=cc_ao[h].ap()[:, :]), 16)
        if upto == "p3":
            _finish(nc, blk, s, y)
            return nc

        gT_sb = mt("gT_sb", [128, HT * T], F32, A_GT)
        dsg = [mt(f"dsg{i}", [128, T], F32, A_DSG + 4096 * i) for i in range(2)]
        p4 = ExitStack()
        g_ps = [p4.enter_context(nc.psum_tensor(f"gps{i}", [128, T], F32)) for i in range(2)]

        @blk.gpsimd
        def _(g):
            for ht in range(2, HT):
                g.wait_ge(s.pe_p4.h, ht - 1)
                d_wd[ht % 2].inc(g.dma_start(out=dbuf[ht % 2][:, :], in_=dsw[ht, :, :]), 16)
                wv_x[("wd", ht)] = d_wd[ht % 2].v

        @blk.tensor
        def _(t):
            t.wait_ge(s.act_exp.h, SC_TOT)
            t.wait_ge(s.dve_att.h, 32)
            t.wait_ge(s.d_hdnr.h, 16 * 34)
            C_ORDER = list(range(0, 28)) + list(range(38, IT)) + list(range(28, 38))
            for ht in range(HT):
                t.wait_ge(d_wd[ht % 2].h, wv_x[("wd", ht)])
                if ht >= 2:
                    t.wait_ge(s.act_p4.h, ht - 1)
                    t.wait_ge(s.dve_p4.h, ht - 1)
                b = ht % 2
                last = None
                co = (C_ORDER if ht != HT - 1 else
                      list(range(28, 38)) + list(range(0, 28)) + list(range(38, IT)))
                for ci, c in enumerate(co):
                    if ht == 0 and ci == 34:
                        t.wait_ge(s.d_hdnr.h, 16 * IT)
                    dl = dbuf[b][:, 128 * c:128 * (c + 1)]
                    for th in range(2):
                        last = t.matmul(
                            g_ps[b][:, 512 * th:512 * th + 512], dl,
                            hdn_all[:, T * c + 512 * th:T * c + 512 * th + 512],
                            start=(ci == 0), stop=(ci == IT - 1), skip_group_check=True)
                    if ht == HT - 1 and ci == 9:
                        s.pe_sc.inc(last)
                s.pe_p4.inc(last)

        @blk.scalar
        def _(sc):
            sc = Chained(sc, s.ch_act, ch_state, 'a')
            for ht in range(HT):
                b = ht % 2
                sc.wait_ge(s.pe_p4.h, ht + 1)
                if ht >= 2:
                    sc.wait_ge(s.dve_p4.h, ht - 1)
                sc.activation(dsg[b][:, :], g_ps[b][:, :], AF.Sigmoid,
                              bias=0.0, scale=ds_sb[:, ht:ht + 1], _sem=s.act_p4)

        @blk.vector
        def _(v):
            v = Chained(v, s.ch_dve, ch_state, 'v')
            for ht in range(HT):
                b = ht % 2
                v.wait_ge(s.act_p4.h, ht + 1)
                v.scalar_tensor_tensor(
                    gT_sb[:, T * ht:T * (ht + 1)], g_ps[b][:, :],
                    ds_sb[:, ht:ht + 1], dsg[b][:, :], ALU.mult, ALU.mult, _sem=s.dve_p4)

        p4.close()

        if dbg == "gt":
            @blk.sync
            def _(sy):
                sy.wait_ge(s.dve_p4.h, HT)
                s.d_out.inc(sy.dma_start(out=dbg_out, in_=gT_sb[:, :]), 16)
        if upto == "p4":
            _finish(nc, blk, s, y)
            return nc

        att_sb = mt("att_sb", [128, 16 * T], BF16, A_ATT)
        low_o = mt("low_o", [16, T], BF16, A_LO)
        wo_buf = [mt(f"wo{i}", [128, D], BF16, A_WO + 4096 * i) for i in range(2)]
        sg_sb = mt("sg_sb", [128, HT * T], F32, A_SG)
        p5 = ExitStack()
        lo_ps = p5.enter_context(nc.psum_tensor("lo_ps", [16, T], F32))
        s_ps = [p5.enter_context(nc.psum_tensor(f"sps{i}", [128, 512], F32)) for i in range(2)]

        @blk.sync
        def _(sy):
            sy.wait_ge(s.pe_sc.h, SC_TOT + 1)
            for H in list(range(1, 11)) + [0] + list(range(11, 16)):
                if H == 0:
                    sy.wait_ge(s.pe_p4.h, HT)
                hl, rho = H % 4, H // 4
                sy.wait_ge(cca[hl].h, 1)
                s.d_atr.inc(sy.dma_start(
                    out=att_sb[:, T * H:T * (H + 1)],
                    in_=cc_ao[hl].ap()[128 * rho:128 * (rho + 1), bass.ds(toff_val[0], T)]), 16)

        @blk.gpsimd
        def _(g):
            for oc in range(16):
                if oc < 2:
                    g.wait_ge(s.pe_p4.h, HT)
                else:
                    g.wait_ge(s.pe_p5.h, 2 * (oc - 1))
                d_wo[oc % 2].inc(g.dma_start(out=wo_buf[oc % 2][:, :], in_=wosw[oc, :, :]), 16)
                wv_x[("wo", oc)] = d_wo[oc % 2].v

        @blk.tensor
        def _(t):
            t.wait_ge(s.act_p4.h, HT)
            t.wait_ge(s.dve_p4.h, HT)
            t.wait_ge(s.d_atr.h, 16 * 16)
            last = None
            for th in range(2):
                for j in range(16):
                    last = t.matmul(
                        lo_ps[:, 512 * th:512 * th + 512], Aol[:, RK * j:RK * (j + 1)],
                        att_sb[:, T * j + 512 * th:T * j + 512 * th + 512],
                        start=(j == 0), stop=(j == 15), skip_group_check=True)
            s.pe_lo.inc(last)
            t.wait_ge(s.dve_lo.h, 1)
            pi = 0
            for oc in range(16):
                t.wait_ge(d_wo[oc % 2].h, wv_x[("wo", oc)])
                for th in range(2):
                    ps = s_ps[pi % 2]
                    if pi >= 2:
                        t.wait_ge(s.dve_p5.h, pi - 1)
                    for j in range(16):
                        t.matmul(ps[:, :], wo_buf[oc % 2][:, 128 * j:128 * (j + 1)],
                                 att_sb[:, T * j + 512 * th:T * j + 512 * th + 512],
                                 start=(j == 0), stop=False, skip_group_check=True)
                    s.pe_p5.inc(t.matmul(ps[:, :], beffo[:, 128 * oc:128 * (oc + 1)],
                                         low_o[:, 512 * th:512 * th + 512],
                                         start=False, stop=True, skip_group_check=True))
                    pi += 1

        @blk.vector
        def _(v):
            v = Chained(v, s.ch_dve, ch_state, 'v')
            v.wait_ge(s.pe_lo.h, 1)
            v.tensor_copy(low_o[:, :], lo_ps[:, :], _sem=s.dve_lo)
            pi = 0
            for oc in range(16):
                for th in range(2):
                    ps = s_ps[pi % 2]
                    v.wait_ge(s.pe_p5.h, pi + 1)
                    v.tensor_tensor(
                        sg_sb[:, T * oc + 512 * th:T * oc + 512 * th + 512], ps[:, :],
                        gT_sb[:, T * oc + 512 * th:T * oc + 512 * th + 512], ALU.mult,
                        _sem=s.dve_p5)
                    pi += 1

        p5.close()

        if dbg == "sg":
            @blk.sync
            def _(sy):
                sy.wait_ge(s.dve_p5.h, 32)
                s.d_out.inc(sy.dma_start(out=dbg_out, in_=sg_sb[:, :]), 16)
        if upto == "p5":
            _finish(nc, blk, s, y)
            return nc

        xt2 = [mt(f"xt2_{i}", [128, D], F32, A_XT2 + 8192 * i) for i in range(2)]
        out_sb = [mt(f"osb{i}", [128, D], F32, A_OSB + 8192 * i) for i in range(2)]
        p6 = ExitStack()
        tp_ps = [p6.enter_context(nc.psum_tensor(f"tpps{i}", [128, D], F32)) for i in range(2)]

        OUT0 = sems["d_out"].v

        d_yw = [sems["d_out"], sems["d_atw"]]

        @blk.sync
        def _(sy):
            sy.wait_ge(s.pe_p4.h, HT)
            d_xf[0].inc(sy.dma_start(out=xt2[0][:, :], in_=x_sl[0:128, :]), 16)
            wv_x[("xf", 0)] = d_xf[0].v
            sy.wait_ge(s.pe_p5.h, 32)
            d_xf[1].inc(sy.dma_start(out=xt2[1][:, :], in_=x_sl[128:256, :]), 16)
            wv_x[("xf", 1)] = d_xf[1].v
            for tt in range(8):
                sy.wait_ge(s.dve_p6.h, tt + 1)
                d_yw[tt % 2].inc(sy.dma_start(out=y[128 * tt:128 * (tt + 1), :],
                                              in_=out_sb[tt % 2][:, :]), 16)
                wv_x[("yw", tt)] = d_yw[tt % 2].v
                if tt + 2 < 8:
                    d_xf[tt % 2].inc(sy.dma_start(
                        out=xt2[tt % 2][:, :],
                        in_=x_sl[128 * (tt + 2):128 * (tt + 3), :]), 16)
                    wv_x[("xf", tt + 2)] = d_xf[tt % 2].v
            sy.wait_ge(s.d_out.h, s.d_out.v)
            sy.wait_ge(s.d_atw.h, s.d_atw.v)

        @blk.tensor
        def _(t):
            t.wait_ge(s.dve_p5.h, 32)
            for tt in range(8):
                if tt >= 2:
                    t.wait_ge(s.dve_p6.h, tt - 1)
                last = None
                for dc in range(DC):
                    last = t.transpose(tp_ps[tt % 2][:, 128 * dc:128 * (dc + 1)],
                                       sg_sb[:, T * dc + 128 * tt:T * dc + 128 * (tt + 1)],
                                       ident[:, :])
                s.pe_p6.inc(last)

        @blk.vector
        def _(v):
            v = Chained(v, s.ch_dve, ch_state, 'v')
            v.wait_ge(s.dve_p5.h, 32)
            for tt in range(8):
                v.wait_ge(s.pe_p6.h, tt + 1)
                v.wait_ge(d_xf[tt % 2].h, wv_x[("xf", tt)])
                if tt >= 2:
                    v.wait_ge(d_yw[tt % 2].h, wv_x[("yw", tt - 2)])
                v.tensor_tensor(out_sb[tt % 2][:, :], tp_ps[tt % 2][:, :],
                                xt2[tt % 2][:, :], ALU.add, _sem=s.dve_p6)

        p6.close()

    return nc


def _finish(nc, blk, s, y):
    with nc.sbuf_tensor("dummy_fin", [128, 512], F32) as dummy:
        @blk.vector
        def _(v):
            s.dve_p6.inc(v.memset(dummy[:, :], 0.0))

        @blk.sync
        def _(sy):
            sy.wait_ge(s.dve_p6.h, s.dve_p6.v)
            for tt in range(8):
                for cc in range(4):
                    s.d_out.inc(sy.dma_start(
                        out=y[128 * tt:128 * (tt + 1), 512 * cc:512 * (cc + 1)],
                        in_=dummy[:, :]), 16)
            sy.wait_ge(s.d_out.h, s.d_out.v)



_PERM = np.concatenate([np.arange(0, 128, 2), np.arange(1, 128, 2)])


def _host_prep(inputs):
    x = np.asarray(inputs["L0_attn_x"], dtype=np.float32)
    Wq = np.asarray(inputs["L0_attn_Wq"], dtype=np.float32)
    Wk = np.asarray(inputs["L0_attn_Wk"], dtype=np.float32)
    Wv = np.asarray(inputs["L0_attn_Wv"], dtype=np.float32)
    Wo = np.asarray(inputs["L0_attn_Wo"], dtype=np.float32)
    Aq = np.asarray(inputs["L0_attn_Aq"], dtype=np.float32)
    Ak = np.asarray(inputs["L0_attn_Ak"], dtype=np.float32)
    Av = np.asarray(inputs["L0_attn_Av"], dtype=np.float32)
    Ao = np.asarray(inputs["L0_attn_Ao"], dtype=np.float32)
    Bq = np.asarray(inputs["L0_attn_Bq"], dtype=np.float32)
    Bk = np.asarray(inputs["L0_attn_Bk"], dtype=np.float32)
    Bv = np.asarray(inputs["L0_attn_Bv"], dtype=np.float32)
    Bo = np.asarray(inputs["L0_attn_Bo"], dtype=np.float32)
    gate_w = np.asarray(inputs["L0_ffn_gate_w"], dtype=np.float32)
    up_w = np.asarray(inputs["L0_ffn_up_w"], dtype=np.float32)
    down_w = np.asarray(inputs["L0_ffn_down_w"], dtype=np.float32)
    gate_s = np.asarray(inputs["L0_ffn_gate_s"], dtype=np.float32)
    up_s = np.asarray(inputs["L0_ffn_up_s"], dtype=np.float32)
    down_s = np.asarray(inputs["L0_ffn_down_s"], dtype=np.float32)
    starts = np.asarray(inputs["L0_attn_lora_start_indices"], dtype=np.int32)

    wosw = np.ascontiguousarray(
        Wo.T.reshape(16, 128, 16, 128).transpose(2, 1, 0, 3).reshape(16, 128, D))
    gsw = np.ascontiguousarray(
        gate_w.reshape(IT, 128, 16, 128).transpose(0, 3, 2, 1).reshape(IT, 128, D))
    usw = np.ascontiguousarray(
        up_w.reshape(IT, 128, 16, 128).transpose(0, 3, 2, 1).reshape(IT, 128, D))
    dsw = np.ascontiguousarray(
        down_w.reshape(16, 128, IT, 128).transpose(0, 3, 2, 1).reshape(16, 128, INTER))
    gs_t = np.ascontiguousarray(gate_s.reshape(IT, 128).T)
    us_t = np.ascontiguousarray(up_s.reshape(IT, 128).T)
    ds_t = np.ascontiguousarray(down_s.reshape(16, 128).T)
    aqT = np.ascontiguousarray(Aq.transpose(2, 0, 1).reshape(D, 64))
    akT = np.ascontiguousarray(Ak.transpose(2, 0, 1).reshape(D, 64))
    avT = np.ascontiguousarray(Av.transpose(2, 0, 1).reshape(D, 64))
    aoT = np.ascontiguousarray(Ao.transpose(2, 0, 1).reshape(D, 64))
    boT = np.ascontiguousarray(Bo.transpose(0, 2, 1).reshape(64, D))

    pos = np.arange(S, dtype=np.float64)
    inv = 10000.0 ** (-np.arange(0, 128, 2, dtype=np.float64) / 128.0)
    ang = inv[:, None] * pos[None, :]
    cosT = np.cos(ang).astype(np.float32)
    sinT = np.sin(ang).astype(np.float32)

    jj = np.arange(512)[None, :]
    ii = np.arange(128)[:, None]
    masks = np.stack([(ii <= jj - 128 * m).astype(np.float32) for m in range(4)])
    masks = masks.astype(ml_dtypes.bfloat16)

    ident = np.eye(128, dtype=np.float32)
    s4 = np.tile(np.eye(16, dtype=np.float32), (4, 1))
    rep4 = np.repeat(np.eye(4, dtype=np.float32), 16, axis=1)
    c4row = np.arange(4, dtype=np.float32).reshape(1, 4)

    in_maps = []
    for c in range(8):
        g, r = c // 4, c % 4
        heads = [4 * r + hl for hl in range(NHL)]
        rows_q = np.concatenate([128 * H + _PERM for H in heads])
        rows_v = np.concatenate([128 * H + np.arange(128) for H in heads])
        im = {
            "x_sl": np.ascontiguousarray(x[g, T * r:T * (r + 1), :]),
            "wqT": np.ascontiguousarray(Wq[rows_q, :].T),
            "wkT": np.ascontiguousarray(Wk[rows_q, :].T),
            "wvT": np.ascontiguousarray(Wv[rows_v, :].T),
            "wosw": wosw, "gsw": gsw, "usw": usw, "dsw": dsw,
            "gs_t": gs_t, "us_t": us_t, "ds_t": ds_t,
            "aqT": aqT, "akT": akT, "avT": avT, "aoT": aoT,
            "bqT": np.ascontiguousarray(Bq[:, rows_q, :].transpose(0, 2, 1).reshape(64, 512)),
            "bkT": np.ascontiguousarray(Bk[:, rows_q, :].transpose(0, 2, 1).reshape(64, 512)),
            "bvT": np.ascontiguousarray(Bv[:, rows_v, :].transpose(0, 2, 1).reshape(64, 512)),
            "boT": boT,
            "starts": starts,
            "bsel": np.array([[g + 1.0]], dtype=np.float32),
            "toff": np.array([[T * r]], dtype=np.int32),
            "cosT": cosT, "sinT": sinT, "masks": masks, "ident": ident,
            "s4": s4, "rep4": rep4, "c4row": c4row,
        }
        in_maps.append(im)
    return in_maps


def kernel(**inputs):
    in_maps = _host_prep(inputs)
    nc = build_nc()
    res = run_bass_kernel_spmd(nc, in_maps, core_ids=list(range(8)))
    out = np.empty((2, S, D), dtype=np.float32)
    for c in range(8):
        g, r = c // 4, c % 4
        out[g, T * r:T * (r + 1), :] = np.asarray(res.results[c]["y"], dtype=np.float32)
    return out
